# version 09
# baseline (speedup 1.0000x reference)
import math
import contextlib
import numpy as np
import ml_dtypes
import concourse.bass as bass
import concourse.mybir as mybir
from concourse.aot_env import allow_ici_all_to_all
from concourse.bass_utils import run_bass_kernel_spmd

F32 = mybir.dt.float32
BF16 = mybir.dt.bfloat16
AF = mybir.ActivationFunctionType
ALU = mybir.AluOpType
AXX = mybir.AxisListType.X

NCORE = 8
B_, T_, D_ = 2, 4096, 4096
R_ = 1024
HD, NH, NKVH = 128, 32, 8
DKV = 1024
H_ = 11008
NHC = 86
LCH = 11
SH_ = LCH * 128
EPS = 1e-5
ISQ = 1.0 / math.sqrt(128.0)
GQ = float(2.0 ** 28)

_NF4_CODE = np.array([
    -1.0, -0.6961928009986877, -0.5250730514526367, -0.39491748809814453,
    -0.28444138169288635, -0.18477343022823334, -0.09105003625154495, 0.0,
    0.07958029955625534, 0.16093020141124725, 0.24611230194568634,
    0.33791524171829224, 0.44070982933044434, 0.5626170039176941,
    0.7229568362236023, 1.0], dtype=np.float32)
_NF4_BOUNDS = ((_NF4_CODE[:-1] + _NF4_CODE[1:]) * np.float32(0.5)).astype(np.float32)
OFFC = 2.0
_CP = [float(v) for v in (_NF4_CODE + np.float32(OFFC))]
_GB = [float(v) for v in (_NF4_BOUNDS * np.float32(GQ))]

ENGS = ("sp", "pe", "act", "dve", "pool")


class Sem:
    __slots__ = ("h", "v")
    def __init__(self, h):
        self.h = h
        self.v = 0
    def bump(self, n=1):
        self.v += n
        return self.v


class KB:
    def __init__(self, nc):
        self.nc = nc
        self.prog = {e: [] for e in ENGS}

    def em(self, eng, fn):
        self.prog[eng].append(fn)

    def wait(self, eng, sem, v=None):
        vv = sem.v if v is None else v
        if vv > 0:
            self.em(eng, lambda e, s=sem.h, val=vv: e.wait_ge(s, val))

    def dma(self, eng, out, in_, sem):
        self.em(eng, lambda e, o=out, i=in_, s=sem.h: e.dma_start(out=o, in_=i).then_inc(s, 16))
        return sem.bump(16)

    def op(self, eng, fn, sem=None):
        if sem is None:
            self.em(eng, fn)
            return None
        self.em(eng, lambda e, f=fn, s=sem.h: f(e).then_inc(s, 1))
        return sem.bump(1)

    def chain(self, eng, fn, sem):
        v = self.op(eng, fn, sem)
        self.wait(eng, sem, v)
        return v


def build_program(tap_names=()):
    nc = bass.Bass("TRN2", target_bir_lowering=False, debug=False)

    xT = nc.dram_tensor("xT", [D_, R_], F32, kind="ExternalInput")
    cosT = nc.dram_tensor("cosT", [HD, R_], F32, kind="ExternalInput")
    sinT = nc.dram_tensor("sinT", [HD, R_], F32, kind="ExternalInput")
    maskb = nc.dram_tensor("maskb", [128, 2048], F32, kind="ExternalInput")
    ptm = nc.dram_tensor("ptm", [128, 128], F32, kind="ExternalInput")
    idm = nc.dram_tensor("idm", [128, 128], F32, kind="ExternalInput")
    WqT = nc.dram_tensor("WqT", [D_, D_], F32, kind="ExternalInput")
    WkT = nc.dram_tensor("WkT", [D_, DKV], F32, kind="ExternalInput")
    WvT = nc.dram_tensor("WvT", [D_, DKV], F32, kind="ExternalInput")
    WoT = nc.dram_tensor("WoT", [D_, D_], F32, kind="ExternalInput")
    w1n = nc.dram_tensor("w1n", [SH_, D_], F32, kind="ExternalInput")
    w2n = nc.dram_tensor("w2n", [SH_, D_], F32, kind="ExternalInput")
    w3n = nc.dram_tensor("w3n", [D_, SH_], F32, kind="ExternalInput")
    out = nc.dram_tensor("out", [D_, R_], F32, kind="ExternalOutput")

    qin = nc.dram_tensor("qin", [D_, R_], BF16)
    kin = nc.dram_tensor("kin", [DKV, R_], BF16)
    vin = nc.dram_tensor("vin", [NCORE * R_, HD], BF16)
    qFl = nc.dram_tensor("qFl", [D_, R_], BF16)
    kFl = nc.dram_tensor("kFl", [DKV, R_], BF16)
    vFl = nc.dram_tensor("vFl", [NCORE * R_, HD], BF16)
    dq1in = nc.dram_tensor("dq1in", [LCH * 32 * 128, 128], BF16)
    dq2in = nc.dram_tensor("dq2in", [LCH * 32 * 128, 128], BF16)
    dq3in = nc.dram_tensor("dq3in", [LCH * 32 * 128, 128], BF16)
    dq1F = nc.dram_tensor("dq1F", [NCORE * LCH * 32 * 128, 128], BF16, addr_space="Shared")
    dq2F = nc.dram_tensor("dq2F", [NCORE * LCH * 32 * 128, 128], BF16, addr_space="Shared")
    dq3F = nc.dram_tensor("dq3F", [NCORE * LCH * 32 * 128, 128], BF16, addr_space="Shared")
    oin = nc.dram_tensor("oin", [NCORE * R_, 512], BF16)
    oFl = nc.dram_tensor("oFl", [NCORE * R_, 512], BF16)
    dend = nc.dram_tensor("dend", [64, 512], F32)
    x2d = nc.dram_tensor("x2d", [D_, R_], F32)
    ln2d = nc.dram_tensor("ln2d", [D_, R_], BF16)
    brd = nc.dram_tensor("brd", [D_, R_], BF16)

    interns = dict(qin=qin, kin=kin, vin=vin, dq1in=dq1in, dq2in=dq2in, dq3in=dq3in,
                   oin=oin, dend=dend, x2d=x2d, ln2d=ln2d, brd=brd,
                   qFl=qFl, kFl=kFl, vFl=vFl, oFl=oFl, dq1F=dq1F, dq3F=dq3F)
    taps = {}
    for tn in tap_names:
        tsrc = interns[tn]
        taps[tn] = nc.dram_tensor("tap_" + tn, list(tsrc.ap().shape),
                                  tsrc.ap().dtype, kind="ExternalOutput")

    st = contextlib.ExitStack()
    E = st.enter_context
    with st:
        A = E(nc.sbuf_tensor("A", [128, 44032], BF16))
        Bb = E(nc.sbuf_tensor("Bb", [128, 24576], BF16))
        Cf = E(nc.sbuf_tensor("Cf", [128, 4096], F32))
        Ef = E(nc.sbuf_tensor("Ef", [128, 6144], BF16))
        cos_sb = E(nc.sbuf_tensor("cos_sb", [128, R_], F32))
        sin_sb = E(nc.sbuf_tensor("sin_sb", [128, R_], F32))
        mask_sb = E(nc.sbuf_tensor("mask_sb", [128, 2048], BF16))
        pt_bf = E(nc.sbuf_tensor("pt_bf", [128, 128], BF16))
        id_bf = E(nc.sbuf_tensor("id_bf", [128, 128], BF16))
        onesc_bf = E(nc.sbuf_tensor("onesc_bf", [128, 1], BF16))
        onesr_f = E(nc.sbuf_tensor("onesr_f", [1, 128], F32))
        S1 = E(nc.sbuf_tensor("S1", [1, 2048], F32))
        den_sb = E(nc.sbuf_tensor("den_sb", [64, 512], F32))
        amax_sb = E(nc.sbuf_tensor("amax_sb", [128, 32], F32))
        ramax_sb = E(nc.sbuf_tensor("ramax_sb", [128, 32], F32))
        Tf = E(nc.sbuf_tensor("Tf", [128, 3072], F32))
        tst = E(nc.sbuf_tensor("tst", [128, 512], BF16))

        pA = E(nc.psum_tensor("pA", [128, 1024], F32))
        pB = E(nc.psum_tensor("pB", [128, 1024], F32))
        pC = E(nc.psum_tensor("pC", [128, 512], F32))
        pD = E(nc.psum_tensor("pD", [128, 512], F32))
        pE = E(nc.psum_tensor("pE", [128, 512], F32))
        ptr = E(nc.psum_tensor("ptr", [128, 512], BF16))

        sems = {n: Sem(E(nc.semaphore(n))) for n in
                ("ld", "lp", "stq", "spe", "sact", "sdve",
                 "ckv", "cd1", "cd2", "cd3", "co")}
        ld, lp, stq = sems["ld"], sems["lp"], sems["stq"]
        spe, sact, sdve = sems["spe"], sems["sact"], sems["sdve"]
        ckv, cd1, cd2, cd3, co = (sems[n] for n in ("ckv", "cd1", "cd2", "cd3", "co"))

        k = KB(nc)

        k.dma("sp", cos_sb[:, :], cosT.ap(), ld)
        k.dma("sp", sin_sb[:, :], sinT.ap(), ld)
        k.dma("sp", Cf[:, 0:2048], maskb.ap(), ld)
        k.dma("sp", Cf[:, 2048:2176], ptm.ap(), ld)
        k.dma("sp", Cf[:, 2176:2304], idm.ap(), ld)
        k.wait("dve", ld)
        k.op("dve", lambda e: e.tensor_copy(mask_sb[:, :], Cf[:, 0:2048]), sdve)
        k.op("dve", lambda e: e.tensor_copy(pt_bf[:, :], Cf[:, 2048:2176]), sdve)
        k.op("dve", lambda e: e.tensor_copy(id_bf[:, :], Cf[:, 2176:2304]), sdve)
        k.op("dve", lambda e: e.memset(onesc_bf[:, :], 1.0), sdve)
        k.op("dve", lambda e: e.memset(onesr_f[:, :], 1.0), sdve)
        k.wait("sp", sdve)

        def ln_stats_pass(src_dram):
            k.wait("pool", sdve)
            k.wait("pool", sact)
            k.wait("pool", spe)
            for dc in range(32):
                sl = dc % 2
                xb = Bb[:, sl * 1024:(sl + 1) * 1024]
                sq = Bb[:, 2048 + sl * 1024:2048 + (sl + 1) * 1024]
                if dc >= 2:
                    k.wait("pool", spe)
                k.dma("pool", xb[:, :], src_dram.ap()[dc * 128:(dc + 1) * 128, :], lp)
                k.wait("act", lp)
                if dc >= 2:
                    k.wait("act", spe)
                k.op("act", lambda e, sq=sq, xb=xb: e.activation(sq[:, :], xb[:, :], AF.Square), sact)
                k.wait("pe", sact)
                f, l = (dc == 0), (dc == 31)
                k.op("pe", lambda e, xb=xb, f=f, l=l:
                     e.matmul(pB[0:1, 0:512], onesc_bf[:, :], xb[:, 0:512], start=f, stop=l))
                k.op("pe", lambda e, xb=xb, f=f, l=l:
                     e.matmul(pB[0:1, 512:1024], onesc_bf[:, :], xb[:, 512:1024], start=f, stop=l))
                k.op("pe", lambda e, sq=sq, f=f, l=l:
                     e.matmul(pC[0:1, 0:512], onesc_bf[:, :], sq[:, 0:512], start=f, stop=l))
                k.op("pe", lambda e, sq=sq, f=f, l=l:
                     e.matmul(pD[0:1, 0:512], onesc_bf[:, :], sq[:, 512:1024], start=f, stop=l), spe)

        def ln_rows(tb, sx_ap, ssq_ap):
            r = S1[0:1, tb * 1024 + 0:tb * 1024 + 512]
            mr = S1[0:1, tb * 1024 + 512:tb * 1024 + 1024]
            k.wait("dve", spe)
            k.chain("dve", lambda e, r=r, sx=sx_ap: e.tensor_scalar(r, sx, 1.0 / D_, None, ALU.mult), sdve)
            k.chain("dve", lambda e, mr=mr, r=r: e.tensor_copy(mr, r), sdve)
            k.chain("dve", lambda e, r=r: e.tensor_tensor(r, r, r, ALU.mult), sdve)
            k.chain("dve", lambda e, r=r, sq=ssq_ap:
                    e.scalar_tensor_tensor(r, sq, 1.0 / D_, r, ALU.mult, ALU.subtract), sdve)
            k.chain("dve", lambda e, r=r: e.tensor_scalar(r, r, EPS, None, ALU.add), sdve)
            k.wait("act", sdve)
            k.op("act", lambda e, r=r: e.activation(r, r, AF.Sqrt), sact)
            k.wait("dve", sact)
            k.chain("dve", lambda e, r=r: e.reciprocal(r, r), sdve)
            k.chain("dve", lambda e, mr=mr, r=r: e.tensor_tensor(mr, mr, r, ALU.mult), sdve)

        def ln_rows_both():
            ln_rows(0, pB[0:1, 0:512], pC[0:1, 0:512])
            ln_rows(1, pB[0:1, 512:1024], pD[0:1, 0:512])

        def bcast_rows(tb):
            r = S1[0:1, tb * 1024 + 0:tb * 1024 + 512]
            mr = S1[0:1, tb * 1024 + 512:tb * 1024 + 1024]
            k.wait("pe", sdve)
            k.op("pe", lambda e, r=r: e.matmul(pC[:, :], onesr_f[:, :], r, start=True, stop=True))
            k.op("pe", lambda e, mr=mr: e.matmul(pD[:, :], onesr_f[:, :], mr, start=True, stop=True), spe)

        dq_tiles = []
        for _wsrc, _dqdst, _rows, _swap, _ctiles, _wid in (
                (w1n, dq1in, SH_, False, [(0, 2048), (2048, 2048)], 0),
                (w2n, dq2in, SH_, False, [(0, 2048), (2048, 2048)], 1),
                (w3n, dq3in, D_, True, [(0, SH_)], 2)):
            for _lt in range(_rows // 128):
                for (_c0, _cw) in _ctiles:
                    dq_tiles.append((_wsrc, _dqdst, _lt, _c0, _cw, _swap, _wid))
        dq_last_of = {0: 21, 1: 43, 2: 75}
        dq_ag = {0: (dq1in, dq1F, cd1), 1: (dq2in, dq2F, cd2), 2: (dq3in, dq3F, cd3)}
        dq_wt_used = [0, 0]
        dq_chain_end = {}
        dq_trans_done = {}
        dq_cursor = [0]
        qkv_ptr_evac = [0]

        def emit_dq_a(idx):
            wsrc, dqdst, lt, c0, cw, swap, wid = dq_tiles[idx]
            slot = idx % 2
            wt = Cf[:, slot * 2048:slot * 2048 + cw]
            dqn = Ef[:, 2048 + slot * 2048:2048 + slot * 2048 + cw]
            acc = Ef[:, 0:cw]
            nb = cw // 64
            if dq_wt_used[slot]:
                k.wait("sp", sdve, dq_wt_used[slot])
            k.dma("sp", wt[:, :], wsrc.ap()[lt * 128:(lt + 1) * 128, c0:c0 + cw], ld)
            k.wait("dve", ld)
            k.wait("dve", spe, dq_trans_done.get(idx - 2, 0))
            wt3 = wt[:, :].rearrange("p (b w) -> p b w", w=64)
            k.chain("dve", lambda e, wt3=wt3, nb=nb: e.tensor_reduce(
                amax_sb[:, 0:nb], wt3, AXX, ALU.max, apply_absolute_value=True), sdve)
            k.chain("dve", lambda e, nb=nb: e.tensor_scalar(
                ramax_sb[:, 0:nb], amax_sb[:, 0:nb], 1e-30, None, ALU.max), sdve)
            k.chain("dve", lambda e, nb=nb: e.reciprocal(ramax_sb[:, 0:nb], ramax_sb[:, 0:nb]), sdve)
            k.chain("dve", lambda e, nb=nb: e.tensor_scalar(
                ramax_sb[:, 0:nb], ramax_sb[:, 0:nb], GQ, None, ALU.mult), sdve)
            rb = ramax_sb[:, 0:nb].unsqueeze(2).broadcast_to([128, nb, 64])
            ab = amax_sb[:, 0:nb].unsqueeze(2).broadcast_to([128, nb, 64])
            k.chain("dve", lambda e, wt3=wt3, rb=rb: e.tensor_tensor(wt3, wt3, rb, ALU.mult), sdve)
            k.chain("dve", lambda e, a=acc, w=wt: e.tensor_scalar(
                a, w[:, :], _GB[0], _CP[1], ALU.subtract, ALU.min), sdve)
            k.chain("dve", lambda e, a=acc: e.tensor_scalar(a, a, _CP[0], None, ALU.max), sdve)
            for j in range(2, 16):
                k.chain("dve", lambda e, t=dqn, w=wt, j=j: e.tensor_scalar(
                    t, w[:, :], _GB[j - 1], _CP[j], ALU.subtract, ALU.min), sdve)
                k.chain("dve", lambda e, a=acc, t=dqn: e.tensor_tensor(a, a, t, ALU.max), sdve)
            k.chain("dve", lambda e, d=dqn, a=acc, ab=ab: e.scalar_tensor_tensor(
                d.rearrange("p (b q) -> p b q", q=64), a.rearrange("p (b q) -> p b q", q=64),
                -OFFC, ab, ALU.add, ALU.mult), sdve)
            dq_chain_end[idx] = sdve.v
            dq_wt_used[slot] = sdve.v

        def emit_dq_b(idx):
            wsrc, dqdst, lt, c0, cw, swap, wid = dq_tiles[idx]
            slot = idx % 2
            dqn = Ef[:, 2048 + slot * 2048:2048 + slot * 2048 + cw]
            ntile = cw // 128
            ng = (ntile + 3) // 4
            for g in range(ng):
                cnt = min(4, ntile - g * 4)
                k.wait("pe", sdve, dq_chain_end[idx])
                if qkv_ptr_evac[0]:
                    k.wait("pe", sact, qkv_ptr_evac[0])
                for u in range(cnt):
                    k.op("pe", lambda e, d=dqn, g=g, u=u: e.transpose(
                        ptr[:, u * 128:(u + 1) * 128],
                        d[:, (g * 4 + u) * 128:(g * 4 + u + 1) * 128], id_bf[:, :]),
                        spe if u == cnt - 1 else None)
                k.wait("act", spe)
                k.wait("act", stq)
                k.op("act", lambda e, n=cnt * 128: e.copy(tst[:, 0:n], ptr[:, 0:n]), sact)
                qkv_ptr_evac[0] = sact.v
                k.wait("pool", sact)
                for u in range(cnt):
                    ci = (c0 + (g * 4 + u) * 128) // 128
                    tidx = (ci * 32 + lt) if swap else (lt * 32 + ci)
                    k.dma("pool", dqdst.ap()[tidx * 128:(tidx + 1) * 128, :],
                          tst[:, u * 128:(u + 1) * 128], stq)
            dq_trans_done[idx] = spe.v
            if idx == dq_last_of[wid]:
                din, dgath, csem = dq_ag[wid]
                k.wait("pool", stq)
                k.op("pool", lambda e, i=din, o=dgath: e.collective_compute(
                    "AllGather", ALU.bypass, replica_groups=[list(range(NCORE))],
                    ins=[i.ap()[:, :]], outs=[o.ap()[:, :]]), csem)

        def emit_dq(n):
            for _ in range(n):
                cur = dq_cursor[0]
                if cur >= 2 and cur - 2 < len(dq_tiles):
                    if (cur - 2) not in dq_trans_done:
                        emit_dq_b(cur - 2)
                if cur < len(dq_tiles):
                    emit_dq_a(cur)
                    dq_cursor[0] = cur + 1
                elif cur - 2 >= len(dq_tiles) - 1:
                    break

        def drain_dq():
            while dq_cursor[0] < len(dq_tiles):
                emit_dq(1)
            for idx in range(max(0, len(dq_tiles) - 2), len(dq_tiles)):
                if idx not in dq_trans_done:
                    emit_dq_b(idx)

        ln_stats_pass(xT)
        ln_rows_both()
        for tb in range(2):
            bcast_rows(tb)
            k.wait("sp", sdve)
            k.wait("sp", sact)
            for dc in range(32):
                sl = dc % 2
                xf = Cf[:, sl * 1024:(sl + 1) * 1024]
                t1 = Cf[:, 2048 + sl * 512:2048 + (sl + 1) * 512]
                if dc >= 2 or tb > 0:
                    k.wait("sp", sdve)
                k.dma("sp", xf[:, 0:512],
                      xT.ap()[dc * 128:(dc + 1) * 128, tb * 512:(tb + 1) * 512], ld)
                k.wait("dve", ld)
                k.wait("dve", spe)
                dstv = A[:, dc * 1024 + tb * 512: dc * 1024 + (tb + 1) * 512]
                k.op("dve", lambda e, t1=t1, xf=xf: e.tensor_tensor(t1, xf[:, 0:512], pC[:, :], ALU.mult), sdve)
                k.op("dve", lambda e, d=dstv, t1=t1: e.tensor_tensor(d, t1, pD[:, :], ALU.subtract), sdve)

        wh_free = [0, 0]

        def load_w_pool(Wd, oc, wh):
            base = Wd.ap().rearrange("(c p) q -> p c q", p=128)
            if wh_free[wh]:
                k.wait("pool", spe, wh_free[wh])
            for q4 in range(4):
                wsrc = base[:, q4 * 8:(q4 + 1) * 8, oc * 128:(oc + 1) * 128]
                dstb = Bb[:, wh * 4096 + q4 * 1024:wh * 4096 + (q4 + 1) * 1024]
                k.dma("pool", dstb.rearrange("p (c q) -> p c q", c=8), wsrc, lp)
            return lp.v

        qkv_jobs = [(WqT, oc, "q") for oc in range(32)] + \
                   [(WkT, oc, "k") for oc in range(8)] + \
                   [(WvT, oc, "v") for oc in range(8)]
        pslots = [(pA, 0), (pA, 512), (pB, 0), (pB, 512)]
        slot_free = {}
        qrslot_used = [0, 0]
        rot_evac_l = [0]
        k.wait("sp", sdve)
        k.wait("pe", sdve)
        k.wait("pool", sdve)

        for ji, (Wd, oc, kind) in enumerate(qkv_jobs):
            wh = ji % 2
            wready = load_w_pool(Wd, oc, wh)
            for tb in range(2):
                unit = ji * 2 + tb
                sl = unit % 4
                s2 = unit % 2
                pt_, po_ = pslots[sl]
                ps = pt_[:, po_:po_ + 512]
                k.wait("pe", lp, wready)
                if sl in slot_free:
                    k.wait("pe", sact, slot_free[sl])
                for dc in range(32):
                    lhs = Bb[:, wh * 4096 + dc * 128:wh * 4096 + (dc + 1) * 128]
                    rhs = A[:, dc * 1024 + tb * 512: dc * 1024 + (tb + 1) * 512]
                    k.op("pe", lambda e, l=lhs, r=rhs, ps=ps, f=(dc == 0), la=(dc == 31):
                         e.matmul(ps, l[:, :], r, start=f, stop=la),
                         spe if dc == 31 else None)
                mm_done = spe.v
                stage_bf = Bb[:, 8192 + sl * 512: 8192 + (sl + 1) * 512]
                out_bf = Bb[:, 10240 + sl * 512: 10240 + (sl + 1) * 512]
                if kind in ("q", "k"):
                    qf32 = Tf[:, s2 * 512:(s2 + 1) * 512]
                    rotf = Tf[:, 1024 + s2 * 512:1024 + (s2 + 1) * 512]
                    t1 = Tf[:, 2048:2560]
                    t2 = Tf[:, 2560:3072]
                    k.wait("act", spe, mm_done)
                    if qrslot_used[s2]:
                        k.wait("act", sdve, qrslot_used[s2])
                    k.op("act", lambda e, s=stage_bf, ps=ps: e.copy(s[:, :], ps), sact)
                    cp1 = sact.v
                    k.op("act", lambda e, d=qf32, ps=ps: e.copy(d, ps), sact)
                    slot_free[sl] = sact.v
                    cp2 = sact.v
                    k.wait("pe", sact, cp1)
                    if rot_evac_l[0]:
                        k.wait("pe", sact, rot_evac_l[0])
                    k.op("pe", lambda e, s=stage_bf: e.matmul(pE[:, :], pt_bf[:, :], s[:, :],
                                                              start=True, stop=True), spe)
                    rot_done = spe.v
                    k.wait("act", spe, rot_done)
                    k.op("act", lambda e, d=rotf: e.copy(d, pE[:, :]), sact)
                    rot_evac_l[0] = sact.v
                    cosv = cos_sb[:, tb * 512:(tb + 1) * 512]
                    sinv = sin_sb[:, tb * 512:(tb + 1) * 512]
                    k.wait("dve", sact, cp2)
                    k.op("dve", lambda e, t1=t1, q=qf32, c=cosv: e.tensor_tensor(t1, q, c, ALU.mult), sdve)
                    k.wait("dve", sact, rot_evac_l[0])
                    k.op("dve", lambda e, t2=t2, r=rotf, s=sinv: e.tensor_tensor(t2, r, s, ALU.mult), sdve)
                    k.wait("dve", stq)
                    k.op("dve", lambda e, o=out_bf, t1=t1, t2=t2:
                         e.tensor_tensor(o[:, :], t1, t2, ALU.add), sdve)
                    qrslot_used[s2] = sdve.v
                    dst_dram = qin if kind == "q" else kin
                    k.wait("pool", sdve)
                    k.dma("pool", dst_dram.ap()[oc * 128:(oc + 1) * 128, tb * 512:(tb + 1) * 512],
                          out_bf[:, :], stq)
                else:
                    k.wait("act", spe, mm_done)
                    k.op("act", lambda e, s=stage_bf, ps=ps: e.copy(s[:, :], ps), sact)
                    slot_free[sl] = sact.v
                    k.wait("pe", sact)
                    if qkv_ptr_evac[0]:
                        k.wait("pe", sact, qkv_ptr_evac[0])
                    for tt in range(4):
                        k.op("pe", lambda e, s=stage_bf, tt=tt:
                             e.transpose(ptr[:, tt * 128:(tt + 1) * 128],
                                         s[:, tt * 128:(tt + 1) * 128], id_bf[:, :]),
                             spe if tt == 3 else None)
                    k.wait("act", spe)
                    k.wait("act", stq)
                    k.op("act", lambda e, v=out_bf: e.copy(v[:, :], ptr[:, :]), sact)
                    qkv_ptr_evac[0] = sact.v
                    k.wait("pool", sact)
                    for tt in range(4):
                        k.dma("pool", vin.ap()[oc * 1024 + tb * 512 + tt * 128:
                                               oc * 1024 + tb * 512 + (tt + 1) * 128, :],
                              out_bf[:, tt * 128:(tt + 1) * 128], stq)
            wh_free[wh] = spe.v
            if ji % 2 == 1:
                emit_dq(1)

        k.wait("pool", stq)

        def cc_a2a(ein, eout):
            def f(e, i=ein, o=eout):
                with allow_ici_all_to_all():
                    return e.collective_compute("AllToAll", ALU.bypass,
                                                replica_groups=[list(range(NCORE))],
                                                ins=[i.ap()[:, :]], outs=[o.ap()[:, :]])
            return f
        k.op("pool", cc_a2a(qin, qFl), ckv)
        k.op("pool", cc_a2a(kin, kFl), ckv)
        k.op("pool", cc_a2a(vin, vFl), ckv)

        k.wait("sp", ckv)
        k.wait("sp", stq)
        k.wait("sp", sdve)
        k.wait("pe", ckv)
        k.wait("pe", sdve)
        k.wait("pe", sact)
        fl_pc = [0]
        fl_den = [0]
        fl_recip = [0]
        fl_bcast = [0]
        q4v = Bb[:, 0:2048]
        kpref = Bb[:, 2048:6144]
        vpref = Bb[:, 6144:10240]
        attv = Bb[:, 10240:12288]
        ostage = Bb[:, 12288:12800]

        for b in range(2):
            for qb in range(8):
                nkb = 4 * qb + 4
                nbb = qb + 1
                k.wait("sp", spe)
                k.dma("sp", q4v[:, :].rearrange("p (h q) -> p h q", h=4),
                      qFl.ap()[qb * 512:(qb + 1) * 512, b * 512:(b + 1) * 512]
                      .rearrange("(h p) q -> p h q", p=128), ld)
                for r in range(nbb):
                    k.dma("sp", kpref[:, r * 512:(r + 1) * 512],
                          kFl.ap()[r * 128:(r + 1) * 128, b * 512:(b + 1) * 512], ld)
                    k.dma("sp", vpref[:, r * 512:(r + 1) * 512].rearrange("p (t h) -> p t h", t=4),
                          vFl.ap()[r * 1024 + b * 512:r * 1024 + (b + 1) * 512, :]
                          .rearrange("(t p) h -> p t h", p=128), ld)
                kv_ready = ld.v

                for hh in range(4):
                    inst = b * 4 + hh
                    qslice = q4v[:, hh * 512:(hh + 1) * 512]
                    k.wait("pe", ld, kv_ready)
                    if fl_pc[0]:
                        k.wait("pe", sdve, fl_pc[0])
                    if fl_den[0]:
                        k.wait("pe", sact, fl_den[0])
                    nbat = nkb // 2
                    exp_vals = {}
                    for bb in range(nbat):
                        sps = pA if (bb % 2 == 0) else pB
                        attsl = attv[:, (bb % 2) * 1024:((bb % 2) + 1) * 1024]
                        if bb >= 2:
                            k.wait("pe", sact, exp_vals[bb - 2])
                        for half in range(2):
                            kb = bb * 2 + half
                            lhs = kpref[:, kb * 128:(kb + 1) * 128]
                            k.op("pe", lambda e, l=lhs, q=qslice, sps=sps, half=half:
                                 e.matmul(sps[:, half * 512:(half + 1) * 512], l[:, :], q,
                                          start=True, stop=True),
                                 spe if half == 1 else None)
                        smm_done = spe.v
                        k.wait("act", spe, smm_done)
                        k.op("act", lambda e, a=attsl, sps=sps:
                             e.activation(a, sps[:, :], AF.Exp, scale=ISQ), sact)
                        exp_vals[bb] = sact.v
                        diag = (bb >= nbat - 2)
                        if diag:
                            moff = (bb - (nbat - 2)) * 1024
                            k.wait("dve", sact, exp_vals[bb])
                            k.op("dve", lambda e, a=attsl, m=mask_sb[:, moff:moff + 1024]:
                                 e.tensor_tensor(a, a, m, ALU.mult), sdve)
                            k.wait("pe", sdve)
                        else:
                            k.wait("pe", sact, exp_vals[bb])
                        for half in range(2):
                            kb = bb * 2 + half
                            vt = vpref[:, kb * 128:(kb + 1) * 128]
                            k.op("pe", lambda e, vt=vt, a=attsl, half=half, kb=kb, nkb=nkb:
                                 e.matmul(pC[:, :], vt[:, :], a[:, half * 512:(half + 1) * 512],
                                          start=(kb == 0), stop=(kb == nkb - 1)))
                            k.op("pe", lambda e, a=attsl, half=half, kb=kb, nkb=nkb:
                                 e.matmul(pD[0:1, 0:512], onesc_bf[:, :],
                                          a[:, half * 512:(half + 1) * 512],
                                          start=(kb == 0), stop=(kb == nkb - 1)),
                                 spe if half == 1 else None)
                    av_done = spe.v
                    k.wait("act", spe, av_done)
                    if fl_recip[0]:
                        k.wait("act", sdve, fl_recip[0])
                    k.wait("act", stq)
                    k.op("act", lambda e: e.copy(ostage[:, :], pC[:, :]), sact)
                    k.op("act", lambda e: e.copy(S1[0:1, 0:512], pD[0:1, 0:512]), sact)
                    fl_den[0] = sact.v
                    k.wait("dve", sact, fl_den[0])
                    if fl_bcast[0]:
                        k.wait("dve", spe, fl_bcast[0])
                    k.chain("dve", lambda e: e.reciprocal(S1[0:1, 1536:2048],
                                                          S1[0:1, 0:512]), sdve)
                    fl_recip[0] = sdve.v
                    k.wait("pe", sdve, fl_recip[0])
                    k.op("pe", lambda e: e.matmul(pE[:, :], onesr_f[:, :], S1[0:1, 1536:2048],
                                                  start=True, stop=True), spe)
                    fl_bcast[0] = spe.v
                    k.wait("dve", spe, fl_bcast[0])
                    k.op("dve", lambda e: e.tensor_tensor(ostage[:, :], ostage[:, :], pE[:, :],
                                                          ALU.mult), sdve)
                    fl_pc[0] = sdve.v
                    k.wait("pool", sdve, fl_pc[0])
                    orow = qb * 1024 + b * 512 + hh * 128
                    k.dma("pool", oin.ap()[orow:orow + 128, :], ostage[:, :], stq)
                    emit_dq(1)

        drain_dq()
        k.wait("pool", stq)
        k.op("pool", cc_a2a(oin, oFl), co)

        k.wait("pool", co)
        for dc in range(32):
            for b in range(2):
                srow = (dc // 4) * 1024 + b * 512 + (dc % 4) * 128
                k.dma("pool", A[:, dc * 1024 + b * 512: dc * 1024 + (b + 1) * 512],
                      oFl.ap()[srow:srow + 128, :], lp)
        ocat_ready = lp.v
        k.wait("pe", lp, ocat_ready)

        wo_slots = [(pA, 0), (pA, 512), (pE, 0)]
        wslot_free = {}
        wh_free[:] = [0, 0]
        for oc in range(32):
            wh = oc % 2
            wready = load_w_pool(WoT, oc, wh)
            for tb in range(2):
                sl = (oc * 2 + tb) % 3
                pt_, po_ = wo_slots[sl]
                ps = pt_[:, po_:po_ + 512]
                k.wait("pe", lp, wready)
                if sl in wslot_free:
                    k.wait("pe", sact, wslot_free[sl])
                for dc in range(32):
                    lhs = Bb[:, wh * 4096 + dc * 128:wh * 4096 + (dc + 1) * 128]
                    rhs = A[:, dc * 1024 + tb * 512: dc * 1024 + (tb + 1) * 512]
                    k.op("pe", lambda e, l=lhs, r=rhs, ps=ps, f=(dc == 0), la=(dc == 31):
                         e.matmul(ps, l[:, :], r, start=f, stop=la),
                         spe if dc == 31 else None)
                mm_done = spe.v
                stage_bf = Bb[:, 8192 + sl * 1024: 8192 + sl * 1024 + 512]
                sq = Bb[:, 8192 + sl * 1024 + 512: 8192 + (sl + 1) * 1024]
                k.wait("act", spe, mm_done)
                k.wait("act", stq)
                k.op("act", lambda e, s=stage_bf, ps=ps: e.copy(s[:, :], ps), sact)
                k.op("act", lambda e, sq=sq, ps=ps: e.activation(sq[:, :], ps, AF.Square), sact)
                wslot_free[sl] = sact.v
                k.wait("pe", sact)
                sxrow = pB[0:1, tb * 512:(tb + 1) * 512]
                sqrow = (pC if tb == 0 else pD)[0:1, 0:512]
                k.op("pe", lambda e, s=stage_bf, sx=sxrow, f=(oc == 0), la=(oc == 31):
                     e.matmul(sx, onesc_bf[:, :], s[:, :], start=f, stop=la))
                k.op("pe", lambda e, sq=sq, sr=sqrow, f=(oc == 0), la=(oc == 31):
                     e.matmul(sr, onesc_bf[:, :], sq[:, :], start=f, stop=la), spe)
                k.wait("pool", sact)
                k.dma("pool", brd.ap()[oc * 128:(oc + 1) * 128, tb * 512:(tb + 1) * 512],
                      stage_bf[:, :], stq)
            wh_free[wh] = spe.v
        ln_rows_both()

        def branch_apply(branch_dram, resid_dram, dst_dram):
            for tb in range(2):
                bcast_rows(tb)
                k.wait("sp", sdve)
                k.wait("sp", sact)
                for dc in range(32):
                    sl = dc % 2
                    aob = Bb[:, sl * 512:(sl + 1) * 512]
                    xf = Cf[:, sl * 1024:(sl + 1) * 1024]
                    t1 = Cf[:, 2048 + sl * 512:2048 + (sl + 1) * 512]
                    outv = Cf[:, 3072 + sl * 512:3072 + (sl + 1) * 512]
                    if dc >= 2 or tb > 0:
                        k.wait("sp", sdve)
                    k.dma("sp", aob[:, :], branch_dram.ap()[dc * 128:(dc + 1) * 128,
                                                            tb * 512:(tb + 1) * 512], ld)
                    k.dma("sp", xf[:, 0:512], resid_dram.ap()[dc * 128:(dc + 1) * 128,
                                                              tb * 512:(tb + 1) * 512], ld)
                    k.wait("dve", ld)
                    k.wait("dve", spe)
                    k.op("dve", lambda e, t1=t1, a=aob: e.tensor_tensor(t1, a[:, :], pC[:, :], ALU.mult), sdve)
                    k.op("dve", lambda e, t1=t1: e.tensor_tensor(t1, t1, pD[:, :], ALU.subtract), sdve)
                    k.wait("dve", stq)
                    k.op("dve", lambda e, o=outv, t1=t1, xf=xf:
                         e.tensor_tensor(o, t1, xf[:, 0:512], ALU.add), sdve)
                    k.wait("pool", sdve)
                    k.dma("pool", dst_dram.ap()[dc * 128:(dc + 1) * 128, tb * 512:(tb + 1) * 512],
                          outv[:, :], stq)
        branch_apply(brd, xT, x2d)

        k.wait("sp", stq)
        ln_stats_pass(x2d)
        ln_rows_both()
        for tb in range(2):
            bcast_rows(tb)
            k.wait("sp", sdve)
            k.wait("sp", sact)
            for dc in range(32):
                sl = dc % 2
                xf = Cf[:, sl * 1024:(sl + 1) * 1024]
                t1 = Cf[:, 2048 + sl * 512:2048 + (sl + 1) * 512]
                dstb = Bb[:, sl * 512:(sl + 1) * 512]
                if dc >= 2 or tb > 0:
                    k.wait("sp", sdve)
                k.dma("sp", xf[:, 0:512], x2d.ap()[dc * 128:(dc + 1) * 128,
                                                   tb * 512:(tb + 1) * 512], ld)
                k.wait("dve", ld)
                k.wait("dve", spe)
                k.op("dve", lambda e, t1=t1, xf=xf: e.tensor_tensor(t1, xf[:, 0:512], pC[:, :], ALU.mult), sdve)
                k.wait("dve", stq)
                k.op("dve", lambda e, d=dstb, t1=t1: e.tensor_tensor(d[:, :], t1, pD[:, :], ALU.subtract), sdve)
                k.wait("pool", sdve)
                k.dma("pool", ln2d.ap()[dc * 128:(dc + 1) * 128, tb * 512:(tb + 1) * 512],
                      dstb[:, :], stq)

        for s_ in ("sp", "pe", "pool"):
            k.wait(s_, cd1)
            k.wait(s_, cd2)
            k.wait(s_, cd3)
        for tb in range(2):
            k.wait("sp", spe)
            k.wait("sp", stq)
            k.wait("sp", sdve)
            k.wait("sp", sact)
            for dc in range(32):
                k.dma("sp", Bb[:, dc * 512:(dc + 1) * 512],
                      ln2d.ap()[dc * 128:(dc + 1) * 128, tb * 512:(tb + 1) * 512], ld)
            ln2_ready = ld.v
            k.wait("pe", ld, ln2_ready)
            k.wait("dve", spe)
            k.wait("pool", spe)
            d1b = Bb[:, 16384:20480]
            d2b = Bb[:, 20480:24576]
            q_ready = {}
            q_freed = {}
            for hc in range(NHC):
                rk, lc = hc // LCH, hc % LCH
                base = (rk * LCH * 32 + lc * 32) * 128
                for q4 in range(4):
                    if hc >= 1:
                        k.wait("pool", spe, q_freed[(hc - 1, q4)])
                    for wsrc, dstb in ((dq1F, d1b), (dq2F, d2b)):
                        srcap = wsrc.ap()[base + q4 * 8 * 128:base + (q4 + 1) * 8 * 128, :] \
                            .rearrange("(c p) q -> p c q", p=128)
                        k.dma("pool", dstb[:, q4 * 1024:(q4 + 1) * 1024]
                              .rearrange("p (c q) -> p c q", c=8), srcap, lp)
                    q_ready[(hc, q4)] = lp.v
                gh = hc % 2
                gps = pA[:, gh * 512:(gh + 1) * 512]
                ups = pB[:, gh * 512:(gh + 1) * 512]
                if hc >= 2:
                    k.wait("pe", sdve)
                for dc in range(32):
                    if dc % 8 == 0:
                        k.wait("pe", lp, q_ready[(hc, dc // 8)])
                    l1 = d1b[:, dc * 128:(dc + 1) * 128]
                    l2 = d2b[:, dc * 128:(dc + 1) * 128]
                    rhs = Bb[:, dc * 512:(dc + 1) * 512]
                    k.op("pe", lambda e, l=l1, r=rhs, ps=gps, f=(dc == 0), la=(dc == 31):
                         e.matmul(ps, l[:, :], r, start=f, stop=la))
                    k.op("pe", lambda e, l=l2, r=rhs, ps=ups, f=(dc == 0), la=(dc == 31):
                         e.matmul(ps, l[:, :], r, start=f, stop=la),
                         spe if (dc % 8 == 7) else None)
                    if dc % 8 == 7:
                        q_freed[(hc, dc // 8)] = spe.v
                k.wait("act", spe)
                sg = Ef[:, gh * 512:(gh + 1) * 512]
                k.op("act", lambda e, sg=sg, g=gps: e.activation(sg[:, :], g, AF.Silu), sact)
                k.wait("dve", sact)
                hv = A[:, hc * 512:(hc + 1) * 512]
                k.op("dve", lambda e, h=hv, u=ups, sg=sg:
                     e.tensor_tensor(h, u, sg[:, :], ALU.mult), sdve)
            hact_done = sdve.v
            k.wait("pe", sdve, hact_done)
            down_slots = [(pA, 0), (pA, 512), (pE, 0)]
            dslot_free = {}
            oc_ready = {}
            oc_freed = {}
            base_free = spe.v

            def emit_down_load(oc):
                sbase = (oc % 2) * 11264
                k.wait("pool", spe, oc_freed.get(oc - 2, base_free))
                for rk in range(NCORE):
                    nlc = LCH if rk < 7 else (NHC - 7 * LCH)
                    srcap = bass.AP(dq3F, (rk * LCH * 32 + oc) * 128 * 128,
                                    [[128, 128], [32 * 128 * 128, nlc], [1, 128]])
                    dst = Bb[:, sbase + rk * LCH * 128:sbase + (rk * LCH + nlc) * 128] \
                        .rearrange("p (c q) -> p c q", c=nlc)
                    k.dma("pool", dst, srcap, lp)
                oc_ready[oc] = lp.v

            emit_down_load(0)
            emit_down_load(1)
            for oc in range(32):
                sbase = (oc % 2) * 11264
                sl = oc % 3
                pt_, po_ = down_slots[sl]
                ps = pt_[:, po_:po_ + 512]
                k.wait("pe", lp, oc_ready[oc])
                if sl in dslot_free:
                    k.wait("pe", sact, dslot_free[sl])
                for hc in range(NHC):
                    lhs = Bb[:, sbase + hc * 128:sbase + (hc + 1) * 128]
                    rhs = A[:, hc * 512:(hc + 1) * 512]
                    k.op("pe", lambda e, l=lhs, r=rhs, ps=ps, f=(hc == 0), la=(hc == NHC - 1):
                         e.matmul(ps, l[:, :], r, start=f, stop=la),
                         spe if hc == NHC - 1 else None)
                oc_freed[oc] = spe.v
                if oc + 2 < 32:
                    emit_down_load(oc + 2)
                stage_bf = Ef[:, sl * 1024:sl * 1024 + 512]
                sq = Ef[:, sl * 1024 + 512:(sl + 1) * 1024]
                k.wait("act", spe, oc_freed[oc])
                k.wait("act", stq)
                k.op("act", lambda e, s=stage_bf, ps=ps: e.copy(s[:, :], ps), sact)
                k.op("act", lambda e, sq=sq, ps=ps: e.activation(sq[:, :], ps, AF.Square), sact)
                dslot_free[sl] = sact.v
                k.wait("pe", sact)
                sxrow = pB[0:1, tb * 512:(tb + 1) * 512]
                sqrow = (pC if tb == 0 else pD)[0:1, 0:512]
                k.op("pe", lambda e, s=stage_bf, sx=sxrow, f=(oc == 0), la=(oc == 31):
                     e.matmul(sx, onesc_bf[:, :], s[:, :], start=f, stop=la))
                k.op("pe", lambda e, sq=sq, sr=sqrow, f=(oc == 0), la=(oc == 31):
                     e.matmul(sr, onesc_bf[:, :], sq[:, :], start=f, stop=la), spe)
                k.wait("pool", sact)
                k.dma("pool", brd.ap()[oc * 128:(oc + 1) * 128, tb * 512:(tb + 1) * 512],
                      stage_bf[:, :], stq)
            ln_rows(tb, pB[0:1, tb * 512:(tb + 1) * 512], (pC if tb == 0 else pD)[0:1, 0:512])
            bcast_rows(tb)
            k.wait("sp", sdve)
            k.wait("sp", sact)
            for dc in range(32):
                sl = dc % 2
                yb = Ef[:, 3072 + sl * 512:3072 + (sl + 1) * 512]
                xf = Cf[:, sl * 1024:(sl + 1) * 1024]
                t1 = Cf[:, 2048 + sl * 512:2048 + (sl + 1) * 512]
                outv = Cf[:, 3072 + sl * 512:3072 + (sl + 1) * 512]
                if dc >= 2:
                    k.wait("sp", sdve)
                k.wait("sp", stq)
                k.dma("sp", yb[:, :], brd.ap()[dc * 128:(dc + 1) * 128,
                                               tb * 512:(tb + 1) * 512], ld)
                k.dma("sp", xf[:, 0:512], x2d.ap()[dc * 128:(dc + 1) * 128,
                                                   tb * 512:(tb + 1) * 512], ld)
                k.wait("dve", ld)
                k.wait("dve", spe)
                k.op("dve", lambda e, t1=t1, a=yb: e.tensor_tensor(t1, a[:, :], pC[:, :], ALU.mult), sdve)
                k.op("dve", lambda e, t1=t1: e.tensor_tensor(t1, t1, pD[:, :], ALU.subtract), sdve)
                k.wait("dve", stq)
                k.op("dve", lambda e, o=outv, t1=t1, xf=xf:
                     e.tensor_tensor(o, t1, xf[:, 0:512], ALU.add), sdve)
                k.wait("pool", sdve)
                k.dma("pool", out.ap()[dc * 128:(dc + 1) * 128, tb * 512:(tb + 1) * 512],
                      outv[:, :], stq)

        for tn, tdst in taps.items():
            tsrc = interns[tn]
            k.wait("pool", stq)
            k.dma("pool", tdst.ap()[:, :].rearrange("a b -> (a b)").unsqueeze(0),
                  tsrc.ap()[:, :].rearrange("a b -> (a b)").unsqueeze(0), stq)
        k.wait("pool", stq)
        k.wait("sp", stq)

        with nc.Block() as block:
            @block.sync
            def _(eng):
                for fn in k.prog["sp"]:
                    fn(eng)

            @block.tensor
            def _(eng):
                for fn in k.prog["pe"]:
                    fn(eng)

            @block.scalar
            def _(eng):
                for fn in k.prog["act"]:
                    fn(eng)

            @block.vector
            def _(eng):
                for fn in k.prog["dve"]:
                    fn(eng)

            @block.gpsimd
            def _(eng):
                for fn in k.prog["pool"]:
                    fn(eng)

    return nc


def _rope_tables(positions):
    base = 500000.0
    inv_freq = 1.0 / base ** (np.arange(0, HD, 2, dtype=np.float32) / HD)
    factor, low_f, high_f = 32.0, 1.0, 4.0
    old_ctx = 8192
    low_wl, high_wl = old_ctx / low_f, old_ctx / high_f
    wavelen = 2 * math.pi / inv_freq
    inv_llama = np.where(wavelen > low_wl, inv_freq / factor, inv_freq)
    smooth = (old_ctx / wavelen - low_f) / (high_f - low_f)
    smoothed = (1 - smooth) * inv_llama / factor + smooth * inv_llama
    is_med = ~(wavelen < high_wl) & ~(wavelen > low_wl)
    invf = np.where(is_med, smoothed, inv_llama).astype(np.float32)
    pos = positions.astype(np.float32)
    freqs = pos[:, None] * invf[None, :]
    emb = np.concatenate([freqs, freqs], axis=-1)
    return (np.ascontiguousarray(np.cos(emb).astype(np.float32).T),
            np.ascontiguousarray(np.sin(emb).astype(np.float32).T))


def make_in_maps(L0_attn_x, L0_attn_Wq, L0_attn_Wk, L0_attn_Wv, L0_attn_Wo,
                 L0_ffn_w1, L0_ffn_w2, L0_ffn_w3):
    x = np.asarray(L0_attn_x, dtype=np.float32)
    WqTh = np.ascontiguousarray(np.asarray(L0_attn_Wq, np.float32).T)
    WkTh = np.ascontiguousarray(np.asarray(L0_attn_Wk, np.float32).T)
    WvTh = np.ascontiguousarray(np.asarray(L0_attn_Wv, np.float32).T)
    WoTh = np.ascontiguousarray(np.asarray(L0_attn_Wo, np.float32).T)
    w1 = np.asarray(L0_ffn_w1, np.float32)
    w2 = np.asarray(L0_ffn_w2, np.float32)
    w3 = np.asarray(L0_ffn_w3, np.float32)

    m = np.zeros((4, 128, 512), np.float32)
    for o in range(4):
        for kp in range(128):
            m[o, kp, :] = (np.arange(512) >= (o * 128 + kp)).astype(np.float32)
    maskb_h = np.ascontiguousarray(m.transpose(1, 0, 2).reshape(128, 2048))

    P = np.zeros((128, 128), np.float32)
    for i in range(64):
        P[i, i + 64] = -1.0
        P[i + 64, i] = 1.0
    ptm_h = np.ascontiguousarray(P.T)
    idm_h = np.eye(128, dtype=np.float32)

    in_maps = []
    for c in range(NCORE):
        tsl = slice(c * 512, (c + 1) * 512)
        xTc = np.ascontiguousarray(
            np.concatenate([x[0, tsl], x[1, tsl]], axis=0).T)
        tpos = np.concatenate([np.arange(c * 512, (c + 1) * 512)] * 2)
        cosc, sinc = _rope_tables(tpos)
        h0, h1 = c * SH_, min((c + 1) * SH_, H_)
        w1c = np.zeros((SH_, D_), np.float32); w1c[:h1 - h0] = w1[h0:h1]
        w2c = np.zeros((SH_, D_), np.float32); w2c[:h1 - h0] = w2[h0:h1]
        w3c = np.zeros((D_, SH_), np.float32); w3c[:, :h1 - h0] = w3[:, h0:h1]
        in_maps.append(dict(
            xT=xTc, cosT=cosc, sinT=sinc,
            maskb=maskb_h, ptm=ptm_h, idm=idm_h,
            WqT=WqTh, WkT=WkTh, WvT=WvTh, WoT=WoTh,
            w1n=w1c, w2n=w2c, w3n=w3c))
    return in_maps


def kernel(L0_attn_x, L0_attn_Wq, L0_attn_Wk, L0_attn_Wv, L0_attn_Wo,
           L0_ffn_w1, L0_ffn_w2, L0_ffn_w3):
    in_maps = make_in_maps(L0_attn_x, L0_attn_Wq, L0_attn_Wk, L0_attn_Wv, L0_attn_Wo,
                           L0_ffn_w1, L0_ffn_w2, L0_ffn_w3)
    nc = build_program()
    res = run_bass_kernel_spmd(nc, in_maps, core_ids=list(range(NCORE)))
    full = np.empty((B_, T_, D_), np.float32)
    for c in range(NCORE):
        oc = np.asarray(res.results[c]["out"])
        full[0, c * 512:(c + 1) * 512] = oc[:, 0:512].T
        full[1, c * 512:(c + 1) * 512] = oc[:, 512:1024].T
    return full


# version 11
# speedup vs baseline: 1.0391x; 1.0391x over previous
import math
import contextlib
import numpy as np
import ml_dtypes
import concourse.bass as bass
import concourse.mybir as mybir
from concourse.aot_env import allow_ici_all_to_all
from concourse.bass_utils import run_bass_kernel_spmd

F32 = mybir.dt.float32
BF16 = mybir.dt.bfloat16
AF = mybir.ActivationFunctionType
ALU = mybir.AluOpType
AXX = mybir.AxisListType.X

NCORE = 8
B_, T_, D_ = 2, 4096, 4096
R_ = 1024
HD, NH, NKVH = 128, 32, 8
DKV = 1024
H_ = 11008
NHC = 86
LCH = 11
SH_ = LCH * 128
EPS = 1e-5
ISQ = 1.0 / math.sqrt(128.0)
GQ = float(2.0 ** 28)

_NF4_CODE = np.array([
    -1.0, -0.6961928009986877, -0.5250730514526367, -0.39491748809814453,
    -0.28444138169288635, -0.18477343022823334, -0.09105003625154495, 0.0,
    0.07958029955625534, 0.16093020141124725, 0.24611230194568634,
    0.33791524171829224, 0.44070982933044434, 0.5626170039176941,
    0.7229568362236023, 1.0], dtype=np.float32)
_NF4_BOUNDS = ((_NF4_CODE[:-1] + _NF4_CODE[1:]) * np.float32(0.5)).astype(np.float32)
OFFC = 2.0
_CP = [float(v) for v in (_NF4_CODE + np.float32(OFFC))]
_GB = [float(v) for v in (_NF4_BOUNDS * np.float32(GQ))]

ENGS = ("sp", "pe", "act", "dve", "pool")


class Sem:
    __slots__ = ("h", "v")
    def __init__(self, h):
        self.h = h
        self.v = 0
    def bump(self, n=1):
        self.v += n
        return self.v


class KB:
    def __init__(self, nc):
        self.nc = nc
        self.prog = {e: [] for e in ENGS}

    def em(self, eng, fn):
        self.prog[eng].append(fn)

    def wait(self, eng, sem, v=None):
        vv = sem.v if v is None else v
        if vv > 0:
            self.em(eng, lambda e, s=sem.h, val=vv: e.wait_ge(s, val))

    def dma(self, eng, out, in_, sem):
        self.em(eng, lambda e, o=out, i=in_, s=sem.h: e.dma_start(out=o, in_=i).then_inc(s, 16))
        return sem.bump(16)

    def op(self, eng, fn, sem=None):
        if sem is None:
            self.em(eng, fn)
            return None
        self.em(eng, lambda e, f=fn, s=sem.h: f(e).then_inc(s, 1))
        return sem.bump(1)

    def chain(self, eng, fn, sem):
        v = self.op(eng, fn, sem)
        self.wait(eng, sem, v)
        return v


def build_program(tap_names=()):
    nc = bass.Bass("TRN2", target_bir_lowering=False, debug=False)

    xT = nc.dram_tensor("xT", [D_, R_], F32, kind="ExternalInput")
    cosT = nc.dram_tensor("cosT", [HD, R_], F32, kind="ExternalInput")
    sinT = nc.dram_tensor("sinT", [HD, R_], F32, kind="ExternalInput")
    maskb = nc.dram_tensor("maskb", [128, 2048], F32, kind="ExternalInput")
    ptm = nc.dram_tensor("ptm", [128, 128], F32, kind="ExternalInput")
    idm = nc.dram_tensor("idm", [128, 128], F32, kind="ExternalInput")
    WqT = nc.dram_tensor("WqT", [D_, D_], F32, kind="ExternalInput")
    WkT = nc.dram_tensor("WkT", [D_, DKV], F32, kind="ExternalInput")
    WvT = nc.dram_tensor("WvT", [D_, DKV], F32, kind="ExternalInput")
    WoT = nc.dram_tensor("WoT", [D_, D_], F32, kind="ExternalInput")
    w1n = nc.dram_tensor("w1n", [SH_, D_], F32, kind="ExternalInput")
    w2n = nc.dram_tensor("w2n", [SH_, D_], F32, kind="ExternalInput")
    w3n = nc.dram_tensor("w3n", [D_, SH_], F32, kind="ExternalInput")
    out = nc.dram_tensor("out", [D_, R_], F32, kind="ExternalOutput")

    qin = nc.dram_tensor("qin", [D_, R_], BF16)
    kin = nc.dram_tensor("kin", [DKV, R_], BF16)
    vin = nc.dram_tensor("vin", [NCORE * R_, HD], BF16)
    qFl = nc.dram_tensor("qFl", [D_, R_], BF16)
    kFl = nc.dram_tensor("kFl", [DKV, R_], BF16)
    vFl = nc.dram_tensor("vFl", [NCORE * R_, HD], BF16)
    dq1in = nc.dram_tensor("dq1in", [LCH * 32 * 128, 128], BF16)
    dq2in = nc.dram_tensor("dq2in", [LCH * 32 * 128, 128], BF16)
    dq3in = nc.dram_tensor("dq3in", [LCH * 32 * 128, 128], BF16)
    dq1F = nc.dram_tensor("dq1F", [NCORE * LCH * 32 * 128, 128], BF16, addr_space="Shared")
    dq2F = nc.dram_tensor("dq2F", [NCORE * LCH * 32 * 128, 128], BF16, addr_space="Shared")
    dq3F = nc.dram_tensor("dq3F", [NCORE * LCH * 32 * 128, 128], BF16, addr_space="Shared")
    oin = nc.dram_tensor("oin", [NCORE * R_, 512], BF16)
    oFl = nc.dram_tensor("oFl", [NCORE * R_, 512], BF16)
    dend = nc.dram_tensor("dend", [64, 512], F32)
    x2d = nc.dram_tensor("x2d", [D_, R_], F32)
    ln2d = nc.dram_tensor("ln2d", [D_, R_], BF16)
    brd = nc.dram_tensor("brd", [D_, R_], BF16)

    interns = dict(qin=qin, kin=kin, vin=vin, dq1in=dq1in, dq2in=dq2in, dq3in=dq3in,
                   oin=oin, dend=dend, x2d=x2d, ln2d=ln2d, brd=brd,
                   qFl=qFl, kFl=kFl, vFl=vFl, oFl=oFl, dq1F=dq1F, dq3F=dq3F)
    taps = {}
    for tn in tap_names:
        tsrc = interns[tn]
        taps[tn] = nc.dram_tensor("tap_" + tn, list(tsrc.ap().shape),
                                  tsrc.ap().dtype, kind="ExternalOutput")

    st = contextlib.ExitStack()
    E = st.enter_context
    with st:
        A = E(nc.sbuf_tensor("A", [128, 44032], BF16))
        Bb = E(nc.sbuf_tensor("Bb", [128, 24576], BF16))
        Cf = E(nc.sbuf_tensor("Cf", [128, 4096], F32))
        Ef = E(nc.sbuf_tensor("Ef", [128, 6144], BF16))
        cos_sb = E(nc.sbuf_tensor("cos_sb", [128, R_], F32))
        sin_sb = E(nc.sbuf_tensor("sin_sb", [128, R_], F32))
        mask_sb = E(nc.sbuf_tensor("mask_sb", [128, 2048], BF16))
        pt_bf = E(nc.sbuf_tensor("pt_bf", [128, 128], BF16))
        id_bf = E(nc.sbuf_tensor("id_bf", [128, 128], BF16))
        onesc_bf = E(nc.sbuf_tensor("onesc_bf", [128, 1], BF16))
        onesr_f = E(nc.sbuf_tensor("onesr_f", [1, 128], F32))
        S1 = E(nc.sbuf_tensor("S1", [1, 2048], F32))
        den_sb = E(nc.sbuf_tensor("den_sb", [64, 512], F32))
        amax_sb = E(nc.sbuf_tensor("amax_sb", [128, 32], F32))
        ramax_sb = E(nc.sbuf_tensor("ramax_sb", [128, 32], F32))
        Tf = E(nc.sbuf_tensor("Tf", [128, 3072], F32))
        tst = E(nc.sbuf_tensor("tst", [128, 512], BF16))

        pA = E(nc.psum_tensor("pA", [128, 1024], F32))
        pB = E(nc.psum_tensor("pB", [128, 1024], F32))
        pC = E(nc.psum_tensor("pC", [128, 512], F32))
        pD = E(nc.psum_tensor("pD", [128, 512], F32))
        pE = E(nc.psum_tensor("pE", [128, 512], F32))
        ptr = E(nc.psum_tensor("ptr", [128, 512], BF16))

        sems = {n: Sem(E(nc.semaphore(n))) for n in
                ("ld", "lp", "stq", "spe", "sact", "sdve",
                 "ckv", "cd1", "cd2", "cd3", "co")}
        ld, lp, stq = sems["ld"], sems["lp"], sems["stq"]
        spe, sact, sdve = sems["spe"], sems["sact"], sems["sdve"]
        ckv, cd1, cd2, cd3, co = (sems[n] for n in ("ckv", "cd1", "cd2", "cd3", "co"))

        k = KB(nc)

        k.dma("sp", cos_sb[:, :], cosT.ap(), ld)
        k.dma("sp", sin_sb[:, :], sinT.ap(), ld)
        k.dma("sp", Cf[:, 0:2048], maskb.ap(), ld)
        k.dma("sp", Cf[:, 2048:2176], ptm.ap(), ld)
        k.dma("sp", Cf[:, 2176:2304], idm.ap(), ld)
        k.wait("dve", ld)
        k.op("dve", lambda e: e.tensor_copy(mask_sb[:, :], Cf[:, 0:2048]), sdve)
        k.op("dve", lambda e: e.tensor_copy(pt_bf[:, :], Cf[:, 2048:2176]), sdve)
        k.op("dve", lambda e: e.tensor_copy(id_bf[:, :], Cf[:, 2176:2304]), sdve)
        k.op("dve", lambda e: e.memset(onesc_bf[:, :], 1.0), sdve)
        k.op("dve", lambda e: e.memset(onesr_f[:, :], 1.0), sdve)
        k.wait("sp", sdve)

        def ln_stats_pass(src_dram):
            k.wait("pool", sdve)
            k.wait("pool", sact)
            k.wait("pool", spe)
            for dc in range(32):
                sl = dc % 2
                xb = Bb[:, sl * 1024:(sl + 1) * 1024]
                sq = Bb[:, 2048 + sl * 1024:2048 + (sl + 1) * 1024]
                if dc >= 2:
                    k.wait("pool", spe)
                k.dma("pool", xb[:, :], src_dram.ap()[dc * 128:(dc + 1) * 128, :], lp)
                k.wait("act", lp)
                if dc >= 2:
                    k.wait("act", spe)
                k.op("act", lambda e, sq=sq, xb=xb: e.activation(sq[:, :], xb[:, :], AF.Square), sact)
                k.wait("pe", sact)
                f, l = (dc == 0), (dc == 31)
                k.op("pe", lambda e, xb=xb, f=f, l=l:
                     e.matmul(pB[0:1, 0:512], onesc_bf[:, :], xb[:, 0:512], start=f, stop=l))
                k.op("pe", lambda e, xb=xb, f=f, l=l:
                     e.matmul(pB[0:1, 512:1024], onesc_bf[:, :], xb[:, 512:1024], start=f, stop=l))
                k.op("pe", lambda e, sq=sq, f=f, l=l:
                     e.matmul(pC[0:1, 0:512], onesc_bf[:, :], sq[:, 0:512], start=f, stop=l))
                k.op("pe", lambda e, sq=sq, f=f, l=l:
                     e.matmul(pD[0:1, 0:512], onesc_bf[:, :], sq[:, 512:1024], start=f, stop=l), spe)

        def ln_rows(tb, sx_ap, ssq_ap):
            r = S1[0:1, tb * 1024 + 0:tb * 1024 + 512]
            mr = S1[0:1, tb * 1024 + 512:tb * 1024 + 1024]
            k.wait("dve", spe)
            k.chain("dve", lambda e, r=r, sx=sx_ap: e.tensor_scalar(r, sx, 1.0 / D_, None, ALU.mult), sdve)
            k.chain("dve", lambda e, mr=mr, r=r: e.tensor_copy(mr, r), sdve)
            k.chain("dve", lambda e, r=r: e.tensor_tensor(r, r, r, ALU.mult), sdve)
            k.chain("dve", lambda e, r=r, sq=ssq_ap:
                    e.scalar_tensor_tensor(r, sq, 1.0 / D_, r, ALU.mult, ALU.subtract), sdve)
            k.chain("dve", lambda e, r=r: e.tensor_scalar(r, r, EPS, None, ALU.add), sdve)
            k.wait("act", sdve)
            k.op("act", lambda e, r=r: e.activation(r, r, AF.Sqrt), sact)
            k.wait("dve", sact)
            k.chain("dve", lambda e, r=r: e.reciprocal(r, r), sdve)
            k.chain("dve", lambda e, mr=mr, r=r: e.tensor_tensor(mr, mr, r, ALU.mult), sdve)

        def ln_rows_both():
            ln_rows(0, pB[0:1, 0:512], pC[0:1, 0:512])
            ln_rows(1, pB[0:1, 512:1024], pD[0:1, 0:512])

        def bcast_rows(tb):
            r = S1[0:1, tb * 1024 + 0:tb * 1024 + 512]
            mr = S1[0:1, tb * 1024 + 512:tb * 1024 + 1024]
            k.wait("pe", sdve)
            k.op("pe", lambda e, r=r: e.matmul(pC[:, :], onesr_f[:, :], r, start=True, stop=True))
            k.op("pe", lambda e, mr=mr: e.matmul(pD[:, :], onesr_f[:, :], mr, start=True, stop=True), spe)

        dq_tiles = []
        for _wsrc, _dqdst, _rows, _swap, _ctiles, _wid in (
                (w1n, dq1in, SH_, False, [(0, 2048), (2048, 2048)], 0),
                (w2n, dq2in, SH_, False, [(0, 2048), (2048, 2048)], 1),
                (w3n, dq3in, D_, True, [(0, SH_)], 2)):
            for _lt in range(_rows // 128):
                for (_c0, _cw) in _ctiles:
                    dq_tiles.append((_wsrc, _dqdst, _lt, _c0, _cw, _swap, _wid))
        dq_last_of = {0: 21, 1: 43, 2: 75}
        dq_ag = {0: (dq1in, dq1F, cd1), 1: (dq2in, dq2F, cd2), 2: (dq3in, dq3F, cd3)}
        dq_wt_used = [0, 0]
        dq_chain_end = {}
        dq_trans_done = {}
        dq_cursor = [0]
        qkv_ptr_evac = [0]

        def emit_dq_a(idx):
            wsrc, dqdst, lt, c0, cw, swap, wid = dq_tiles[idx]
            slot = idx % 2
            wt = Cf[:, slot * 2048:slot * 2048 + cw]
            dqn = Ef[:, 2048 + slot * 2048:2048 + slot * 2048 + cw]
            acc = Ef[:, 0:cw]
            nb = cw // 64
            if dq_wt_used[slot]:
                k.wait("sp", sdve, dq_wt_used[slot])
            k.dma("sp", wt[:, :], wsrc.ap()[lt * 128:(lt + 1) * 128, c0:c0 + cw], ld)
            k.wait("dve", ld)
            k.wait("dve", spe, dq_trans_done.get(idx - 2, 0))
            wt3 = wt[:, :].rearrange("p (b w) -> p b w", w=64)
            k.chain("dve", lambda e, wt3=wt3, nb=nb: e.tensor_reduce(
                amax_sb[:, 0:nb], wt3, AXX, ALU.max, apply_absolute_value=True), sdve)
            k.chain("dve", lambda e, nb=nb: e.tensor_scalar(
                ramax_sb[:, 0:nb], amax_sb[:, 0:nb], 1e-30, None, ALU.max), sdve)
            k.chain("dve", lambda e, nb=nb: e.reciprocal(ramax_sb[:, 0:nb], ramax_sb[:, 0:nb]), sdve)
            k.chain("dve", lambda e, nb=nb: e.tensor_scalar(
                ramax_sb[:, 0:nb], ramax_sb[:, 0:nb], GQ, None, ALU.mult), sdve)
            rb = ramax_sb[:, 0:nb].unsqueeze(2).broadcast_to([128, nb, 64])
            ab = amax_sb[:, 0:nb].unsqueeze(2).broadcast_to([128, nb, 64])
            k.chain("dve", lambda e, wt3=wt3, rb=rb: e.tensor_tensor(wt3, wt3, rb, ALU.mult), sdve)
            k.chain("dve", lambda e, a=acc, w=wt: e.tensor_scalar(
                a, w[:, :], _GB[0], _CP[1], ALU.subtract, ALU.min), sdve)
            k.chain("dve", lambda e, a=acc: e.tensor_scalar(a, a, _CP[0], None, ALU.max), sdve)
            for j in range(2, 16):
                k.chain("dve", lambda e, t=dqn, w=wt, j=j: e.tensor_scalar(
                    t, w[:, :], _GB[j - 1], _CP[j], ALU.subtract, ALU.min), sdve)
                k.chain("dve", lambda e, a=acc, t=dqn: e.tensor_tensor(a, a, t, ALU.max), sdve)
            k.chain("dve", lambda e, d=dqn, a=acc, ab=ab: e.scalar_tensor_tensor(
                d.rearrange("p (b q) -> p b q", q=64), a.rearrange("p (b q) -> p b q", q=64),
                -OFFC, ab, ALU.add, ALU.mult), sdve)
            dq_chain_end[idx] = sdve.v
            dq_wt_used[slot] = sdve.v

        def emit_dq_b(idx):
            wsrc, dqdst, lt, c0, cw, swap, wid = dq_tiles[idx]
            slot = idx % 2
            dqn = Ef[:, 2048 + slot * 2048:2048 + slot * 2048 + cw]
            ntile = cw // 128
            ng = (ntile + 3) // 4
            for g in range(ng):
                cnt = min(4, ntile - g * 4)
                k.wait("pe", sdve, dq_chain_end[idx])
                if qkv_ptr_evac[0]:
                    k.wait("pe", sact, qkv_ptr_evac[0])
                for u in range(cnt):
                    k.op("pe", lambda e, d=dqn, g=g, u=u: e.transpose(
                        ptr[:, u * 128:(u + 1) * 128],
                        d[:, (g * 4 + u) * 128:(g * 4 + u + 1) * 128], id_bf[:, :]),
                        spe if u == cnt - 1 else None)
                k.wait("act", spe)
                k.wait("act", stq)
                k.op("act", lambda e, n=cnt * 128: e.copy(tst[:, 0:n], ptr[:, 0:n]), sact)
                qkv_ptr_evac[0] = sact.v
                k.wait("pool", sact)
                for u in range(cnt):
                    ci = (c0 + (g * 4 + u) * 128) // 128
                    tidx = (ci * 32 + lt) if swap else (lt * 32 + ci)
                    k.dma("pool", dqdst.ap()[tidx * 128:(tidx + 1) * 128, :],
                          tst[:, u * 128:(u + 1) * 128], stq)
            dq_trans_done[idx] = spe.v
            if idx == dq_last_of[wid]:
                din, dgath, csem = dq_ag[wid]
                k.wait("pool", stq)
                k.op("pool", lambda e, i=din, o=dgath: e.collective_compute(
                    "AllGather", ALU.bypass, replica_groups=[list(range(NCORE))],
                    ins=[i.ap()[:, :]], outs=[o.ap()[:, :]]), csem)

        def emit_dq(n):
            for _ in range(n):
                cur = dq_cursor[0]
                if cur >= 2 and cur - 2 < len(dq_tiles):
                    if (cur - 2) not in dq_trans_done:
                        emit_dq_b(cur - 2)
                if cur < len(dq_tiles):
                    emit_dq_a(cur)
                    dq_cursor[0] = cur + 1
                elif cur - 2 >= len(dq_tiles) - 1:
                    break

        def drain_dq():
            while dq_cursor[0] < len(dq_tiles):
                emit_dq(1)
            for idx in range(max(0, len(dq_tiles) - 2), len(dq_tiles)):
                if idx not in dq_trans_done:
                    emit_dq_b(idx)

        ln_stats_pass(xT)
        ln_rows_both()
        for tb in range(2):
            bcast_rows(tb)
            k.wait("sp", sdve)
            k.wait("sp", sact)
            for dc in range(32):
                sl = dc % 2
                xf = Cf[:, sl * 1024:(sl + 1) * 1024]
                t1 = Cf[:, 2048 + sl * 512:2048 + (sl + 1) * 512]
                if dc >= 2 or tb > 0:
                    k.wait("sp", sdve)
                k.dma("sp", xf[:, 0:512],
                      xT.ap()[dc * 128:(dc + 1) * 128, tb * 512:(tb + 1) * 512], ld)
                k.wait("dve", ld)
                k.wait("dve", spe)
                dstv = A[:, dc * 1024 + tb * 512: dc * 1024 + (tb + 1) * 512]
                k.op("dve", lambda e, t1=t1, xf=xf: e.tensor_tensor(t1, xf[:, 0:512], pC[:, :], ALU.mult), sdve)
                k.op("dve", lambda e, d=dstv, t1=t1: e.tensor_tensor(d, t1, pD[:, :], ALU.subtract), sdve)

        wh_free = [0, 0]

        def load_w_pool(Wd, oc, wh):
            base = Wd.ap().rearrange("(c p) q -> p c q", p=128)
            if wh_free[wh]:
                k.wait("pool", spe, wh_free[wh])
            for q4 in range(4):
                wsrc = base[:, q4 * 8:(q4 + 1) * 8, oc * 128:(oc + 1) * 128]
                dstb = Bb[:, wh * 4096 + q4 * 1024:wh * 4096 + (q4 + 1) * 1024]
                k.dma("pool", dstb.rearrange("p (c q) -> p c q", c=8), wsrc, lp)
            return lp.v

        stage_used = [0, 0]
        wstg = Bb[:, 12288:16384].bitcast(F32)

        def load_w_hw(Wd, oc, wh):
            base = Wd.ap().rearrange("(c p) q -> p c q", p=128)
            if wh_free[wh]:
                k.wait("act", spe, wh_free[wh])
            for q4 in range(4):
                sl = q4 % 2
                stg = wstg[:, sl * 1024:(sl + 1) * 1024]
                if stage_used[sl]:
                    k.wait("sp", sact, stage_used[sl])
                wsrc = base[:, q4 * 8:(q4 + 1) * 8, oc * 128:(oc + 1) * 128]
                dv = k.dma("sp", stg.rearrange("p (c q) -> p c q", c=8), wsrc, ld)
                k.wait("act", ld, dv)
                dstb = Bb[:, wh * 4096 + q4 * 1024:wh * 4096 + (q4 + 1) * 1024]
                k.op("act", lambda e, d=dstb, s=stg: e.copy(d[:, :], s), sact)
                stage_used[sl] = sact.v
            return sact.v

        qkv_jobs = [(WqT, oc, "q") for oc in range(32)] + \
                   [(WkT, oc, "k") for oc in range(8)] + \
                   [(WvT, oc, "v") for oc in range(8)]
        pslots = [(pA, 0), (pA, 512), (pB, 0), (pB, 512)]
        slot_free = {}
        qrslot_used = [0, 0]
        rot_evac_l = [0]
        k.wait("sp", sdve)
        k.wait("pe", sdve)
        k.wait("pool", sdve)

        qkv_ready = {0: load_w_hw(qkv_jobs[0][0], qkv_jobs[0][1], 0)}
        for ji, (Wd, oc, kind) in enumerate(qkv_jobs):
            wh = ji % 2
            wready = qkv_ready.pop(ji)
            if ji + 1 < len(qkv_jobs):
                W2, oc2, _ = qkv_jobs[ji + 1]
                qkv_ready[ji + 1] = load_w_hw(W2, oc2, (ji + 1) % 2)
            for tb in range(2):
                unit = ji * 2 + tb
                sl = unit % 4
                s2 = unit % 2
                pt_, po_ = pslots[sl]
                ps = pt_[:, po_:po_ + 512]
                k.wait("pe", sact, wready)
                if sl in slot_free:
                    k.wait("pe", sact, slot_free[sl])
                for dc in range(32):
                    lhs = Bb[:, wh * 4096 + dc * 128:wh * 4096 + (dc + 1) * 128]
                    rhs = A[:, dc * 1024 + tb * 512: dc * 1024 + (tb + 1) * 512]
                    k.op("pe", lambda e, l=lhs, r=rhs, ps=ps, f=(dc == 0), la=(dc == 31):
                         e.matmul(ps, l[:, :], r, start=f, stop=la),
                         spe if dc == 31 else None)
                mm_done = spe.v
                stage_bf = Bb[:, 8192 + sl * 512: 8192 + (sl + 1) * 512]
                out_bf = Bb[:, 10240 + sl * 512: 10240 + (sl + 1) * 512]
                if kind in ("q", "k"):
                    qf32 = Tf[:, s2 * 512:(s2 + 1) * 512]
                    rotf = Tf[:, 1024 + s2 * 512:1024 + (s2 + 1) * 512]
                    t1 = Tf[:, 2048:2560]
                    t2 = Tf[:, 2560:3072]
                    k.wait("act", spe, mm_done)
                    if qrslot_used[s2]:
                        k.wait("act", sdve, qrslot_used[s2])
                    k.op("act", lambda e, s=stage_bf, ps=ps: e.copy(s[:, :], ps), sact)
                    cp1 = sact.v
                    k.op("act", lambda e, d=qf32, ps=ps: e.copy(d, ps), sact)
                    slot_free[sl] = sact.v
                    cp2 = sact.v
                    k.wait("pe", sact, cp1)
                    if rot_evac_l[0]:
                        k.wait("pe", sact, rot_evac_l[0])
                    k.op("pe", lambda e, s=stage_bf: e.matmul(pE[:, :], pt_bf[:, :], s[:, :],
                                                              start=True, stop=True), spe)
                    rot_done = spe.v
                    k.wait("act", spe, rot_done)
                    k.op("act", lambda e, d=rotf: e.copy(d, pE[:, :]), sact)
                    rot_evac_l[0] = sact.v
                    cosv = cos_sb[:, tb * 512:(tb + 1) * 512]
                    sinv = sin_sb[:, tb * 512:(tb + 1) * 512]
                    k.wait("dve", sact, cp2)
                    k.op("dve", lambda e, t1=t1, q=qf32, c=cosv: e.tensor_tensor(t1, q, c, ALU.mult), sdve)
                    k.wait("dve", sact, rot_evac_l[0])
                    k.op("dve", lambda e, t2=t2, r=rotf, s=sinv: e.tensor_tensor(t2, r, s, ALU.mult), sdve)
                    k.wait("dve", stq)
                    k.op("dve", lambda e, o=out_bf, t1=t1, t2=t2:
                         e.tensor_tensor(o[:, :], t1, t2, ALU.add), sdve)
                    qrslot_used[s2] = sdve.v
                    dst_dram = qin if kind == "q" else kin
                    k.wait("pool", sdve)
                    k.dma("pool", dst_dram.ap()[oc * 128:(oc + 1) * 128, tb * 512:(tb + 1) * 512],
                          out_bf[:, :], stq)
                else:
                    k.wait("act", spe, mm_done)
                    k.op("act", lambda e, s=stage_bf, ps=ps: e.copy(s[:, :], ps), sact)
                    slot_free[sl] = sact.v
                    k.wait("pe", sact)
                    if qkv_ptr_evac[0]:
                        k.wait("pe", sact, qkv_ptr_evac[0])
                    for tt in range(4):
                        k.op("pe", lambda e, s=stage_bf, tt=tt:
                             e.transpose(ptr[:, tt * 128:(tt + 1) * 128],
                                         s[:, tt * 128:(tt + 1) * 128], id_bf[:, :]),
                             spe if tt == 3 else None)
                    k.wait("act", spe)
                    k.wait("act", stq)
                    k.op("act", lambda e, v=out_bf: e.copy(v[:, :], ptr[:, :]), sact)
                    qkv_ptr_evac[0] = sact.v
                    k.wait("pool", sact)
                    for tt in range(4):
                        k.dma("pool", vin.ap()[oc * 1024 + tb * 512 + tt * 128:
                                               oc * 1024 + tb * 512 + (tt + 1) * 128, :],
                              out_bf[:, tt * 128:(tt + 1) * 128], stq)
            wh_free[wh] = spe.v
            if ji % 2 == 1:
                emit_dq(1)

        k.wait("pool", stq)

        def cc_a2a(ein, eout):
            def f(e, i=ein, o=eout):
                with allow_ici_all_to_all():
                    return e.collective_compute("AllToAll", ALU.bypass,
                                                replica_groups=[list(range(NCORE))],
                                                ins=[i.ap()[:, :]], outs=[o.ap()[:, :]])
            return f
        k.op("pool", cc_a2a(qin, qFl), ckv)
        k.op("pool", cc_a2a(kin, kFl), ckv)
        k.op("pool", cc_a2a(vin, vFl), ckv)

        k.wait("sp", ckv)
        k.wait("sp", stq)
        k.wait("sp", sdve)
        k.wait("pe", ckv)
        k.wait("pe", sdve)
        k.wait("pe", sact)
        fl_pc = [0]
        fl_den = [0]
        fl_recip = [0]
        fl_bcast = [0]
        q4v = Bb[:, 0:2048]
        kpref = Bb[:, 2048:6144]
        vpref = Bb[:, 6144:10240]
        attv = Bb[:, 10240:12288]
        ostage = Bb[:, 12288:12800]

        for b in range(2):
            for qb in range(8):
                nkb = 4 * qb + 4
                nbb = qb + 1
                k.wait("sp", spe)
                k.dma("sp", q4v[:, :].rearrange("p (h q) -> p h q", h=4),
                      qFl.ap()[qb * 512:(qb + 1) * 512, b * 512:(b + 1) * 512]
                      .rearrange("(h p) q -> p h q", p=128), ld)
                for r in range(nbb):
                    k.dma("sp", kpref[:, r * 512:(r + 1) * 512],
                          kFl.ap()[r * 128:(r + 1) * 128, b * 512:(b + 1) * 512], ld)
                    k.dma("sp", vpref[:, r * 512:(r + 1) * 512].rearrange("p (t h) -> p t h", t=4),
                          vFl.ap()[r * 1024 + b * 512:r * 1024 + (b + 1) * 512, :]
                          .rearrange("(t p) h -> p t h", p=128), ld)
                kv_ready = ld.v

                for hh in range(4):
                    inst = b * 4 + hh
                    qslice = q4v[:, hh * 512:(hh + 1) * 512]
                    k.wait("pe", ld, kv_ready)
                    if fl_pc[0]:
                        k.wait("pe", sdve, fl_pc[0])
                    if fl_den[0]:
                        k.wait("pe", sact, fl_den[0])
                    nbat = nkb // 2
                    exp_vals = {}
                    for bb in range(nbat):
                        sps = pA if (bb % 2 == 0) else pB
                        attsl = attv[:, (bb % 2) * 1024:((bb % 2) + 1) * 1024]
                        if bb >= 2:
                            k.wait("pe", sact, exp_vals[bb - 2])
                        for half in range(2):
                            kb = bb * 2 + half
                            lhs = kpref[:, kb * 128:(kb + 1) * 128]
                            k.op("pe", lambda e, l=lhs, q=qslice, sps=sps, half=half:
                                 e.matmul(sps[:, half * 512:(half + 1) * 512], l[:, :], q,
                                          start=True, stop=True),
                                 spe if half == 1 else None)
                        smm_done = spe.v
                        k.wait("act", spe, smm_done)
                        k.op("act", lambda e, a=attsl, sps=sps:
                             e.activation(a, sps[:, :], AF.Exp, scale=ISQ), sact)
                        exp_vals[bb] = sact.v
                        diag = (bb >= nbat - 2)
                        if diag:
                            moff = (bb - (nbat - 2)) * 1024
                            k.wait("dve", sact, exp_vals[bb])
                            k.op("dve", lambda e, a=attsl, m=mask_sb[:, moff:moff + 1024]:
                                 e.tensor_tensor(a, a, m, ALU.mult), sdve)
                            k.wait("pe", sdve)
                        else:
                            k.wait("pe", sact, exp_vals[bb])
                        for half in range(2):
                            kb = bb * 2 + half
                            vt = vpref[:, kb * 128:(kb + 1) * 128]
                            k.op("pe", lambda e, vt=vt, a=attsl, half=half, kb=kb, nkb=nkb:
                                 e.matmul(pC[:, :], vt[:, :], a[:, half * 512:(half + 1) * 512],
                                          start=(kb == 0), stop=(kb == nkb - 1)))
                            k.op("pe", lambda e, a=attsl, half=half, kb=kb, nkb=nkb:
                                 e.matmul(pD[0:1, 0:512], onesc_bf[:, :],
                                          a[:, half * 512:(half + 1) * 512],
                                          start=(kb == 0), stop=(kb == nkb - 1)),
                                 spe if half == 1 else None)
                    av_done = spe.v
                    k.wait("act", spe, av_done)
                    if fl_recip[0]:
                        k.wait("act", sdve, fl_recip[0])
                    k.wait("act", stq)
                    k.op("act", lambda e: e.copy(ostage[:, :], pC[:, :]), sact)
                    k.op("act", lambda e: e.copy(S1[0:1, 0:512], pD[0:1, 0:512]), sact)
                    fl_den[0] = sact.v
                    k.wait("dve", sact, fl_den[0])
                    if fl_bcast[0]:
                        k.wait("dve", spe, fl_bcast[0])
                    k.chain("dve", lambda e: e.reciprocal(S1[0:1, 1536:2048],
                                                          S1[0:1, 0:512]), sdve)
                    fl_recip[0] = sdve.v
                    k.wait("pe", sdve, fl_recip[0])
                    k.op("pe", lambda e: e.matmul(pE[:, :], onesr_f[:, :], S1[0:1, 1536:2048],
                                                  start=True, stop=True), spe)
                    fl_bcast[0] = spe.v
                    k.wait("dve", spe, fl_bcast[0])
                    k.op("dve", lambda e: e.tensor_tensor(ostage[:, :], ostage[:, :], pE[:, :],
                                                          ALU.mult), sdve)
                    fl_pc[0] = sdve.v
                    k.wait("pool", sdve, fl_pc[0])
                    orow = qb * 1024 + b * 512 + hh * 128
                    k.dma("pool", oin.ap()[orow:orow + 128, :], ostage[:, :], stq)
                    emit_dq(1)

        drain_dq()
        k.wait("pool", stq)
        k.op("pool", cc_a2a(oin, oFl), co)

        k.wait("pool", co)
        for dc in range(32):
            for b in range(2):
                srow = (dc // 4) * 1024 + b * 512 + (dc % 4) * 128
                k.dma("pool", A[:, dc * 1024 + b * 512: dc * 1024 + (b + 1) * 512],
                      oFl.ap()[srow:srow + 128, :], lp)
        ocat_ready = lp.v
        k.wait("pe", lp, ocat_ready)

        wo_slots = [(pA, 0), (pA, 512), (pE, 0)]
        wslot_free = {}
        wh_free[:] = [0, 0]
        for oc in range(32):
            wh = oc % 2
            wready = load_w_pool(WoT, oc, wh)
            for tb in range(2):
                sl = (oc * 2 + tb) % 3
                pt_, po_ = wo_slots[sl]
                ps = pt_[:, po_:po_ + 512]
                k.wait("pe", lp, wready)
                if sl in wslot_free:
                    k.wait("pe", sact, wslot_free[sl])
                for dc in range(32):
                    lhs = Bb[:, wh * 4096 + dc * 128:wh * 4096 + (dc + 1) * 128]
                    rhs = A[:, dc * 1024 + tb * 512: dc * 1024 + (tb + 1) * 512]
                    k.op("pe", lambda e, l=lhs, r=rhs, ps=ps, f=(dc == 0), la=(dc == 31):
                         e.matmul(ps, l[:, :], r, start=f, stop=la),
                         spe if dc == 31 else None)
                mm_done = spe.v
                stage_bf = Bb[:, 8192 + sl * 1024: 8192 + sl * 1024 + 512]
                sq = Bb[:, 8192 + sl * 1024 + 512: 8192 + (sl + 1) * 1024]
                k.wait("act", spe, mm_done)
                k.wait("act", stq)
                k.op("act", lambda e, s=stage_bf, ps=ps: e.copy(s[:, :], ps), sact)
                k.op("act", lambda e, sq=sq, ps=ps: e.activation(sq[:, :], ps, AF.Square), sact)
                wslot_free[sl] = sact.v
                k.wait("pe", sact)
                sxrow = pB[0:1, tb * 512:(tb + 1) * 512]
                sqrow = (pC if tb == 0 else pD)[0:1, 0:512]
                k.op("pe", lambda e, s=stage_bf, sx=sxrow, f=(oc == 0), la=(oc == 31):
                     e.matmul(sx, onesc_bf[:, :], s[:, :], start=f, stop=la))
                k.op("pe", lambda e, sq=sq, sr=sqrow, f=(oc == 0), la=(oc == 31):
                     e.matmul(sr, onesc_bf[:, :], sq[:, :], start=f, stop=la), spe)
                k.wait("pool", sact)
                k.dma("pool", brd.ap()[oc * 128:(oc + 1) * 128, tb * 512:(tb + 1) * 512],
                      stage_bf[:, :], stq)
            wh_free[wh] = spe.v
        ln_rows_both()

        def branch_apply(branch_dram, resid_dram, dst_dram):
            for tb in range(2):
                bcast_rows(tb)
                k.wait("sp", sdve)
                k.wait("sp", sact)
                for dc in range(32):
                    sl = dc % 2
                    aob = Bb[:, sl * 512:(sl + 1) * 512]
                    xf = Cf[:, sl * 1024:(sl + 1) * 1024]
                    t1 = Cf[:, 2048 + sl * 512:2048 + (sl + 1) * 512]
                    outv = Cf[:, 3072 + sl * 512:3072 + (sl + 1) * 512]
                    if dc >= 2 or tb > 0:
                        k.wait("sp", sdve)
                    k.dma("sp", aob[:, :], branch_dram.ap()[dc * 128:(dc + 1) * 128,
                                                            tb * 512:(tb + 1) * 512], ld)
                    k.dma("sp", xf[:, 0:512], resid_dram.ap()[dc * 128:(dc + 1) * 128,
                                                              tb * 512:(tb + 1) * 512], ld)
                    k.wait("dve", ld)
                    k.wait("dve", spe)
                    k.op("dve", lambda e, t1=t1, a=aob: e.tensor_tensor(t1, a[:, :], pC[:, :], ALU.mult), sdve)
                    k.op("dve", lambda e, t1=t1: e.tensor_tensor(t1, t1, pD[:, :], ALU.subtract), sdve)
                    k.wait("dve", stq)
                    k.op("dve", lambda e, o=outv, t1=t1, xf=xf:
                         e.tensor_tensor(o, t1, xf[:, 0:512], ALU.add), sdve)
                    k.wait("pool", sdve)
                    k.dma("pool", dst_dram.ap()[dc * 128:(dc + 1) * 128, tb * 512:(tb + 1) * 512],
                          outv[:, :], stq)
        branch_apply(brd, xT, x2d)

        k.wait("sp", stq)
        ln_stats_pass(x2d)
        ln_rows_both()
        for tb in range(2):
            bcast_rows(tb)
            k.wait("sp", sdve)
            k.wait("sp", sact)
            for dc in range(32):
                sl = dc % 2
                xf = Cf[:, sl * 1024:(sl + 1) * 1024]
                t1 = Cf[:, 2048 + sl * 512:2048 + (sl + 1) * 512]
                dstb = Bb[:, sl * 512:(sl + 1) * 512]
                if dc >= 2 or tb > 0:
                    k.wait("sp", sdve)
                k.dma("sp", xf[:, 0:512], x2d.ap()[dc * 128:(dc + 1) * 128,
                                                   tb * 512:(tb + 1) * 512], ld)
                k.wait("dve", ld)
                k.wait("dve", spe)
                k.op("dve", lambda e, t1=t1, xf=xf: e.tensor_tensor(t1, xf[:, 0:512], pC[:, :], ALU.mult), sdve)
                k.wait("dve", stq)
                k.op("dve", lambda e, d=dstb, t1=t1: e.tensor_tensor(d[:, :], t1, pD[:, :], ALU.subtract), sdve)
                k.wait("pool", sdve)
                k.dma("pool", ln2d.ap()[dc * 128:(dc + 1) * 128, tb * 512:(tb + 1) * 512],
                      dstb[:, :], stq)

        for s_ in ("sp", "pe", "pool"):
            k.wait(s_, cd1)
            k.wait(s_, cd2)
            k.wait(s_, cd3)
        for tb in range(2):
            k.wait("sp", spe)
            k.wait("sp", stq)
            k.wait("sp", sdve)
            k.wait("sp", sact)
            for dc in range(32):
                k.dma("sp", Bb[:, dc * 512:(dc + 1) * 512],
                      ln2d.ap()[dc * 128:(dc + 1) * 128, tb * 512:(tb + 1) * 512], ld)
            ln2_ready = ld.v
            k.wait("pe", ld, ln2_ready)
            k.wait("dve", spe)
            k.wait("pool", spe)
            d1b = Bb[:, 16384:20480]
            d2b = Bb[:, 20480:24576]
            q_ready = {}
            q_freed = {}
            for hc in range(NHC):
                rk, lc = hc // LCH, hc % LCH
                base = (rk * LCH * 32 + lc * 32) * 128
                for q4 in range(4):
                    if hc >= 1:
                        k.wait("pool", spe, q_freed[(hc - 1, q4)])
                    for wsrc, dstb in ((dq1F, d1b), (dq2F, d2b)):
                        srcap = wsrc.ap()[base + q4 * 8 * 128:base + (q4 + 1) * 8 * 128, :] \
                            .rearrange("(c p) q -> p c q", p=128)
                        k.dma("pool", dstb[:, q4 * 1024:(q4 + 1) * 1024]
                              .rearrange("p (c q) -> p c q", c=8), srcap, lp)
                    q_ready[(hc, q4)] = lp.v
                gh = hc % 2
                gps = pA[:, gh * 512:(gh + 1) * 512]
                ups = pB[:, gh * 512:(gh + 1) * 512]
                if hc >= 2:
                    k.wait("pe", sdve)
                for dc in range(32):
                    if dc % 8 == 0:
                        k.wait("pe", lp, q_ready[(hc, dc // 8)])
                    l1 = d1b[:, dc * 128:(dc + 1) * 128]
                    l2 = d2b[:, dc * 128:(dc + 1) * 128]
                    rhs = Bb[:, dc * 512:(dc + 1) * 512]
                    k.op("pe", lambda e, l=l1, r=rhs, ps=gps, f=(dc == 0), la=(dc == 31):
                         e.matmul(ps, l[:, :], r, start=f, stop=la))
                    k.op("pe", lambda e, l=l2, r=rhs, ps=ups, f=(dc == 0), la=(dc == 31):
                         e.matmul(ps, l[:, :], r, start=f, stop=la),
                         spe if (dc % 8 == 7) else None)
                    if dc % 8 == 7:
                        q_freed[(hc, dc // 8)] = spe.v
                k.wait("act", spe)
                sg = Ef[:, gh * 512:(gh + 1) * 512]
                k.op("act", lambda e, sg=sg, g=gps: e.activation(sg[:, :], g, AF.Silu), sact)
                k.wait("dve", sact)
                hv = A[:, hc * 512:(hc + 1) * 512]
                k.op("dve", lambda e, h=hv, u=ups, sg=sg:
                     e.tensor_tensor(h, u, sg[:, :], ALU.mult), sdve)
            hact_done = sdve.v
            k.wait("pe", sdve, hact_done)
            down_slots = [(pA, 0), (pA, 512), (pE, 0)]
            dslot_free = {}
            oc_ready = {}
            oc_freed = {}
            base_free = spe.v

            def emit_down_load(oc):
                sbase = (oc % 2) * 11264
                k.wait("pool", spe, oc_freed.get(oc - 2, base_free))
                for rk in range(NCORE):
                    nlc = LCH if rk < 7 else (NHC - 7 * LCH)
                    srcap = bass.AP(dq3F, (rk * LCH * 32 + oc) * 128 * 128,
                                    [[128, 128], [32 * 128 * 128, nlc], [1, 128]])
                    dst = Bb[:, sbase + rk * LCH * 128:sbase + (rk * LCH + nlc) * 128] \
                        .rearrange("p (c q) -> p c q", c=nlc)
                    k.dma("pool", dst, srcap, lp)
                oc_ready[oc] = lp.v

            emit_down_load(0)
            emit_down_load(1)
            for oc in range(32):
                sbase = (oc % 2) * 11264
                sl = oc % 3
                pt_, po_ = down_slots[sl]
                ps = pt_[:, po_:po_ + 512]
                k.wait("pe", lp, oc_ready[oc])
                if sl in dslot_free:
                    k.wait("pe", sact, dslot_free[sl])
                for hc in range(NHC):
                    lhs = Bb[:, sbase + hc * 128:sbase + (hc + 1) * 128]
                    rhs = A[:, hc * 512:(hc + 1) * 512]
                    k.op("pe", lambda e, l=lhs, r=rhs, ps=ps, f=(hc == 0), la=(hc == NHC - 1):
                         e.matmul(ps, l[:, :], r, start=f, stop=la),
                         spe if hc == NHC - 1 else None)
                oc_freed[oc] = spe.v
                if oc + 2 < 32:
                    emit_down_load(oc + 2)
                stage_bf = Ef[:, sl * 1024:sl * 1024 + 512]
                sq = Ef[:, sl * 1024 + 512:(sl + 1) * 1024]
                k.wait("act", spe, oc_freed[oc])
                k.wait("act", stq)
                k.op("act", lambda e, s=stage_bf, ps=ps: e.copy(s[:, :], ps), sact)
                k.op("act", lambda e, sq=sq, ps=ps: e.activation(sq[:, :], ps, AF.Square), sact)
                dslot_free[sl] = sact.v
                k.wait("pe", sact)
                sxrow = pB[0:1, tb * 512:(tb + 1) * 512]
                sqrow = (pC if tb == 0 else pD)[0:1, 0:512]
                k.op("pe", lambda e, s=stage_bf, sx=sxrow, f=(oc == 0), la=(oc == 31):
                     e.matmul(sx, onesc_bf[:, :], s[:, :], start=f, stop=la))
                k.op("pe", lambda e, sq=sq, sr=sqrow, f=(oc == 0), la=(oc == 31):
                     e.matmul(sr, onesc_bf[:, :], sq[:, :], start=f, stop=la), spe)
                k.wait("pool", sact)
                k.dma("pool", brd.ap()[oc * 128:(oc + 1) * 128, tb * 512:(tb + 1) * 512],
                      stage_bf[:, :], stq)
            ln_rows(tb, pB[0:1, tb * 512:(tb + 1) * 512], (pC if tb == 0 else pD)[0:1, 0:512])
            bcast_rows(tb)
            k.wait("sp", sdve)
            k.wait("sp", sact)
            for dc in range(32):
                sl = dc % 2
                yb = Ef[:, 3072 + sl * 512:3072 + (sl + 1) * 512]
                xf = Cf[:, sl * 1024:(sl + 1) * 1024]
                t1 = Cf[:, 2048 + sl * 512:2048 + (sl + 1) * 512]
                outv = Cf[:, 3072 + sl * 512:3072 + (sl + 1) * 512]
                if dc >= 2:
                    k.wait("sp", sdve)
                k.wait("sp", stq)
                k.dma("sp", yb[:, :], brd.ap()[dc * 128:(dc + 1) * 128,
                                               tb * 512:(tb + 1) * 512], ld)
                k.dma("sp", xf[:, 0:512], x2d.ap()[dc * 128:(dc + 1) * 128,
                                                   tb * 512:(tb + 1) * 512], ld)
                k.wait("dve", ld)
                k.wait("dve", spe)
                k.op("dve", lambda e, t1=t1, a=yb: e.tensor_tensor(t1, a[:, :], pC[:, :], ALU.mult), sdve)
                k.op("dve", lambda e, t1=t1: e.tensor_tensor(t1, t1, pD[:, :], ALU.subtract), sdve)
                k.wait("dve", stq)
                k.op("dve", lambda e, o=outv, t1=t1, xf=xf:
                     e.tensor_tensor(o, t1, xf[:, 0:512], ALU.add), sdve)
                k.wait("pool", sdve)
                k.dma("pool", out.ap()[dc * 128:(dc + 1) * 128, tb * 512:(tb + 1) * 512],
                      outv[:, :], stq)

        for tn, tdst in taps.items():
            tsrc = interns[tn]
            k.wait("pool", stq)
            k.dma("pool", tdst.ap()[:, :].rearrange("a b -> (a b)").unsqueeze(0),
                  tsrc.ap()[:, :].rearrange("a b -> (a b)").unsqueeze(0), stq)
        k.wait("pool", stq)
        k.wait("sp", stq)

        with nc.Block() as block:
            @block.sync
            def _(eng):
                for fn in k.prog["sp"]:
                    fn(eng)

            @block.tensor
            def _(eng):
                for fn in k.prog["pe"]:
                    fn(eng)

            @block.scalar
            def _(eng):
                for fn in k.prog["act"]:
                    fn(eng)

            @block.vector
            def _(eng):
                for fn in k.prog["dve"]:
                    fn(eng)

            @block.gpsimd
            def _(eng):
                for fn in k.prog["pool"]:
                    fn(eng)

    return nc


def _rope_tables(positions):
    base = 500000.0
    inv_freq = 1.0 / base ** (np.arange(0, HD, 2, dtype=np.float32) / HD)
    factor, low_f, high_f = 32.0, 1.0, 4.0
    old_ctx = 8192
    low_wl, high_wl = old_ctx / low_f, old_ctx / high_f
    wavelen = 2 * math.pi / inv_freq
    inv_llama = np.where(wavelen > low_wl, inv_freq / factor, inv_freq)
    smooth = (old_ctx / wavelen - low_f) / (high_f - low_f)
    smoothed = (1 - smooth) * inv_llama / factor + smooth * inv_llama
    is_med = ~(wavelen < high_wl) & ~(wavelen > low_wl)
    invf = np.where(is_med, smoothed, inv_llama).astype(np.float32)
    pos = positions.astype(np.float32)
    freqs = pos[:, None] * invf[None, :]
    emb = np.concatenate([freqs, freqs], axis=-1)
    return (np.ascontiguousarray(np.cos(emb).astype(np.float32).T),
            np.ascontiguousarray(np.sin(emb).astype(np.float32).T))


def make_in_maps(L0_attn_x, L0_attn_Wq, L0_attn_Wk, L0_attn_Wv, L0_attn_Wo,
                 L0_ffn_w1, L0_ffn_w2, L0_ffn_w3):
    x = np.asarray(L0_attn_x, dtype=np.float32)
    WqTh = np.ascontiguousarray(np.asarray(L0_attn_Wq, np.float32).T)
    WkTh = np.ascontiguousarray(np.asarray(L0_attn_Wk, np.float32).T)
    WvTh = np.ascontiguousarray(np.asarray(L0_attn_Wv, np.float32).T)
    WoTh = np.ascontiguousarray(np.asarray(L0_attn_Wo, np.float32).T)
    w1 = np.asarray(L0_ffn_w1, np.float32)
    w2 = np.asarray(L0_ffn_w2, np.float32)
    w3 = np.asarray(L0_ffn_w3, np.float32)

    m = np.zeros((4, 128, 512), np.float32)
    for o in range(4):
        for kp in range(128):
            m[o, kp, :] = (np.arange(512) >= (o * 128 + kp)).astype(np.float32)
    maskb_h = np.ascontiguousarray(m.transpose(1, 0, 2).reshape(128, 2048))

    P = np.zeros((128, 128), np.float32)
    for i in range(64):
        P[i, i + 64] = -1.0
        P[i + 64, i] = 1.0
    ptm_h = np.ascontiguousarray(P.T)
    idm_h = np.eye(128, dtype=np.float32)

    in_maps = []
    for c in range(NCORE):
        tsl = slice(c * 512, (c + 1) * 512)
        xTc = np.ascontiguousarray(
            np.concatenate([x[0, tsl], x[1, tsl]], axis=0).T)
        tpos = np.concatenate([np.arange(c * 512, (c + 1) * 512)] * 2)
        cosc, sinc = _rope_tables(tpos)
        h0, h1 = c * SH_, min((c + 1) * SH_, H_)
        w1c = np.zeros((SH_, D_), np.float32); w1c[:h1 - h0] = w1[h0:h1]
        w2c = np.zeros((SH_, D_), np.float32); w2c[:h1 - h0] = w2[h0:h1]
        w3c = np.zeros((D_, SH_), np.float32); w3c[:, :h1 - h0] = w3[:, h0:h1]
        in_maps.append(dict(
            xT=xTc, cosT=cosc, sinT=sinc,
            maskb=maskb_h, ptm=ptm_h, idm=idm_h,
            WqT=WqTh, WkT=WkTh, WvT=WvTh, WoT=WoTh,
            w1n=w1c, w2n=w2c, w3n=w3c))
    return in_maps


def kernel(L0_attn_x, L0_attn_Wq, L0_attn_Wk, L0_attn_Wv, L0_attn_Wo,
           L0_ffn_w1, L0_ffn_w2, L0_ffn_w3):
    in_maps = make_in_maps(L0_attn_x, L0_attn_Wq, L0_attn_Wk, L0_attn_Wv, L0_attn_Wo,
                           L0_ffn_w1, L0_ffn_w2, L0_ffn_w3)
    nc = build_program()
    res = run_bass_kernel_spmd(nc, in_maps, core_ids=list(range(NCORE)))
    full = np.empty((B_, T_, D_), np.float32)
    for c in range(NCORE):
        oc = np.asarray(res.results[c]["out"])
        full[0, c * 512:(c + 1) * 512] = oc[:, 0:512].T
        full[1, c * 512:(c + 1) * 512] = oc[:, 512:1024].T
    return full


# version 12 (final)
# speedup vs baseline: 1.0605x; 1.0206x over previous
import math
import contextlib
import numpy as np
import ml_dtypes
import concourse.bass as bass
import concourse.mybir as mybir
from concourse.aot_env import allow_ici_all_to_all
from concourse.bass_utils import run_bass_kernel_spmd

F32 = mybir.dt.float32
BF16 = mybir.dt.bfloat16
AF = mybir.ActivationFunctionType
ALU = mybir.AluOpType
AXX = mybir.AxisListType.X

NCORE = 8
B_, T_, D_ = 2, 4096, 4096
R_ = 1024
HD, NH, NKVH = 128, 32, 8
DKV = 1024
H_ = 11008
NHC = 86
LCH = 11
SH_ = LCH * 128
EPS = 1e-5
ISQ = 1.0 / math.sqrt(128.0)
GQ = float(2.0 ** 28)

_NF4_CODE = np.array([
    -1.0, -0.6961928009986877, -0.5250730514526367, -0.39491748809814453,
    -0.28444138169288635, -0.18477343022823334, -0.09105003625154495, 0.0,
    0.07958029955625534, 0.16093020141124725, 0.24611230194568634,
    0.33791524171829224, 0.44070982933044434, 0.5626170039176941,
    0.7229568362236023, 1.0], dtype=np.float32)
_NF4_BOUNDS = ((_NF4_CODE[:-1] + _NF4_CODE[1:]) * np.float32(0.5)).astype(np.float32)
OFFC = 2.0
_CP = [float(v) for v in (_NF4_CODE + np.float32(OFFC))]
_GB = [float(v) for v in (_NF4_BOUNDS * np.float32(GQ))]

ENGS = ("sp", "pe", "act", "dve", "pool")


class Sem:
    __slots__ = ("h", "v")
    def __init__(self, h):
        self.h = h
        self.v = 0
    def bump(self, n=1):
        self.v += n
        return self.v


class KB:
    def __init__(self, nc):
        self.nc = nc
        self.prog = {e: [] for e in ENGS}

    def em(self, eng, fn):
        self.prog[eng].append(fn)

    def wait(self, eng, sem, v=None):
        vv = sem.v if v is None else v
        if vv > 0:
            self.em(eng, lambda e, s=sem.h, val=vv: e.wait_ge(s, val))

    def dma(self, eng, out, in_, sem):
        self.em(eng, lambda e, o=out, i=in_, s=sem.h: e.dma_start(out=o, in_=i).then_inc(s, 16))
        return sem.bump(16)

    def op(self, eng, fn, sem=None):
        if sem is None:
            self.em(eng, fn)
            return None
        self.em(eng, lambda e, f=fn, s=sem.h: f(e).then_inc(s, 1))
        return sem.bump(1)

    def chain(self, eng, fn, sem):
        v = self.op(eng, fn, sem)
        self.wait(eng, sem, v)
        return v


def build_program(tap_names=()):
    nc = bass.Bass("TRN2", target_bir_lowering=False, debug=False)

    xT = nc.dram_tensor("xT", [D_, R_], F32, kind="ExternalInput")
    cosT = nc.dram_tensor("cosT", [HD, R_], F32, kind="ExternalInput")
    sinT = nc.dram_tensor("sinT", [HD, R_], F32, kind="ExternalInput")
    maskb = nc.dram_tensor("maskb", [128, 2048], F32, kind="ExternalInput")
    ptm = nc.dram_tensor("ptm", [128, 128], F32, kind="ExternalInput")
    idm = nc.dram_tensor("idm", [128, 128], F32, kind="ExternalInput")
    WqT = nc.dram_tensor("WqT", [D_, D_], F32, kind="ExternalInput")
    WkT = nc.dram_tensor("WkT", [D_, DKV], F32, kind="ExternalInput")
    WvT = nc.dram_tensor("WvT", [D_, DKV], F32, kind="ExternalInput")
    WoT = nc.dram_tensor("WoT", [D_, D_], F32, kind="ExternalInput")
    w1n = nc.dram_tensor("w1n", [SH_, D_], F32, kind="ExternalInput")
    w2n = nc.dram_tensor("w2n", [SH_, D_], F32, kind="ExternalInput")
    w3n = nc.dram_tensor("w3n", [D_, SH_], F32, kind="ExternalInput")
    out = nc.dram_tensor("out", [D_, R_], F32, kind="ExternalOutput")

    qin = nc.dram_tensor("qin", [D_, R_], BF16)
    kin = nc.dram_tensor("kin", [DKV, R_], BF16)
    vin = nc.dram_tensor("vin", [NCORE * R_, HD], BF16)
    qFl = nc.dram_tensor("qFl", [D_, R_], BF16)
    kFl = nc.dram_tensor("kFl", [DKV, R_], BF16)
    vFl = nc.dram_tensor("vFl", [NCORE * R_, HD], BF16)
    dq1in = nc.dram_tensor("dq1in", [LCH * 32 * 128, 128], BF16)
    dq2in = nc.dram_tensor("dq2in", [LCH * 32 * 128, 128], BF16)
    dq3in = nc.dram_tensor("dq3in", [LCH * 32 * 128, 128], BF16)
    dq1F = nc.dram_tensor("dq1F", [NCORE * LCH * 32 * 128, 128], BF16, addr_space="Shared")
    dq2F = nc.dram_tensor("dq2F", [NCORE * LCH * 32 * 128, 128], BF16, addr_space="Shared")
    dq3F = nc.dram_tensor("dq3F", [NCORE * LCH * 32 * 128, 128], BF16, addr_space="Shared")
    oin = nc.dram_tensor("oin", [NCORE * R_, 512], BF16)
    oFl = nc.dram_tensor("oFl", [NCORE * R_, 512], BF16)
    dend = nc.dram_tensor("dend", [64, 512], F32)
    x2d = nc.dram_tensor("x2d", [D_, R_], F32)
    ln2d = nc.dram_tensor("ln2d", [D_, R_], BF16)
    brd = nc.dram_tensor("brd", [D_, R_], BF16)

    interns = dict(qin=qin, kin=kin, vin=vin, dq1in=dq1in, dq2in=dq2in, dq3in=dq3in,
                   oin=oin, dend=dend, x2d=x2d, ln2d=ln2d, brd=brd,
                   qFl=qFl, kFl=kFl, vFl=vFl, oFl=oFl, dq1F=dq1F, dq3F=dq3F)
    taps = {}
    for tn in tap_names:
        tsrc = interns[tn]
        taps[tn] = nc.dram_tensor("tap_" + tn, list(tsrc.ap().shape),
                                  tsrc.ap().dtype, kind="ExternalOutput")

    st = contextlib.ExitStack()
    E = st.enter_context
    with st:
        A = E(nc.sbuf_tensor("A", [128, 44032], BF16))
        Bb = E(nc.sbuf_tensor("Bb", [128, 24576], BF16))
        Cf = E(nc.sbuf_tensor("Cf", [128, 4096], F32))
        Ef = E(nc.sbuf_tensor("Ef", [128, 6144], BF16))
        cos_sb = E(nc.sbuf_tensor("cos_sb", [128, R_], F32))
        sin_sb = E(nc.sbuf_tensor("sin_sb", [128, R_], F32))
        mask_sb = E(nc.sbuf_tensor("mask_sb", [128, 2048], BF16))
        pt_bf = E(nc.sbuf_tensor("pt_bf", [128, 128], BF16))
        id_bf = E(nc.sbuf_tensor("id_bf", [128, 128], BF16))
        onesc_bf = E(nc.sbuf_tensor("onesc_bf", [128, 1], BF16))
        onesr_f = E(nc.sbuf_tensor("onesr_f", [1, 128], F32))
        S1 = E(nc.sbuf_tensor("S1", [1, 2048], F32))
        den_sb = E(nc.sbuf_tensor("den_sb", [64, 512], F32))
        amax_sb = E(nc.sbuf_tensor("amax_sb", [128, 32], F32))
        ramax_sb = E(nc.sbuf_tensor("ramax_sb", [128, 32], F32))
        Tf = E(nc.sbuf_tensor("Tf", [128, 3072], F32))
        tst = E(nc.sbuf_tensor("tst", [128, 512], BF16))

        pA = E(nc.psum_tensor("pA", [128, 1024], F32))
        pB = E(nc.psum_tensor("pB", [128, 1024], F32))
        pC = E(nc.psum_tensor("pC", [128, 512], F32))
        pD = E(nc.psum_tensor("pD", [128, 512], F32))
        pE = E(nc.psum_tensor("pE", [128, 512], F32))
        ptr = E(nc.psum_tensor("ptr", [128, 512], BF16))

        sems = {n: Sem(E(nc.semaphore(n))) for n in
                ("ld", "lp", "stq", "spe", "sact", "sdve",
                 "ckv", "cd1", "cd2", "cd3", "co")}
        ld, lp, stq = sems["ld"], sems["lp"], sems["stq"]
        spe, sact, sdve = sems["spe"], sems["sact"], sems["sdve"]
        ckv, cd1, cd2, cd3, co = (sems[n] for n in ("ckv", "cd1", "cd2", "cd3", "co"))

        k = KB(nc)

        k.dma("sp", cos_sb[:, :], cosT.ap(), ld)
        k.dma("sp", sin_sb[:, :], sinT.ap(), ld)
        k.dma("sp", Cf[:, 0:2048], maskb.ap(), ld)
        k.dma("sp", Cf[:, 2048:2176], ptm.ap(), ld)
        k.dma("sp", Cf[:, 2176:2304], idm.ap(), ld)
        k.wait("dve", ld)
        k.op("dve", lambda e: e.tensor_copy(mask_sb[:, :], Cf[:, 0:2048]), sdve)
        k.op("dve", lambda e: e.tensor_copy(pt_bf[:, :], Cf[:, 2048:2176]), sdve)
        k.op("dve", lambda e: e.tensor_copy(id_bf[:, :], Cf[:, 2176:2304]), sdve)
        k.op("dve", lambda e: e.memset(onesc_bf[:, :], 1.0), sdve)
        k.op("dve", lambda e: e.memset(onesr_f[:, :], 1.0), sdve)
        k.wait("sp", sdve)

        def ln_stats_pass(src_dram):
            k.wait("pool", sdve)
            k.wait("pool", sact)
            k.wait("pool", spe)
            for dc in range(32):
                sl = dc % 2
                xb = Bb[:, sl * 1024:(sl + 1) * 1024]
                sq = Bb[:, 2048 + sl * 1024:2048 + (sl + 1) * 1024]
                if dc >= 2:
                    k.wait("pool", spe)
                k.dma("pool", xb[:, :], src_dram.ap()[dc * 128:(dc + 1) * 128, :], lp)
                k.wait("act", lp)
                if dc >= 2:
                    k.wait("act", spe)
                k.op("act", lambda e, sq=sq, xb=xb: e.activation(sq[:, :], xb[:, :], AF.Square), sact)
                k.wait("pe", sact)
                f, l = (dc == 0), (dc == 31)
                k.op("pe", lambda e, xb=xb, f=f, l=l:
                     e.matmul(pB[0:1, 0:512], onesc_bf[:, :], xb[:, 0:512], start=f, stop=l))
                k.op("pe", lambda e, xb=xb, f=f, l=l:
                     e.matmul(pB[0:1, 512:1024], onesc_bf[:, :], xb[:, 512:1024], start=f, stop=l))
                k.op("pe", lambda e, sq=sq, f=f, l=l:
                     e.matmul(pC[0:1, 0:512], onesc_bf[:, :], sq[:, 0:512], start=f, stop=l))
                k.op("pe", lambda e, sq=sq, f=f, l=l:
                     e.matmul(pD[0:1, 0:512], onesc_bf[:, :], sq[:, 512:1024], start=f, stop=l), spe)

        def ln_rows(tb, sx_ap, ssq_ap):
            r = S1[0:1, tb * 1024 + 0:tb * 1024 + 512]
            mr = S1[0:1, tb * 1024 + 512:tb * 1024 + 1024]
            k.wait("dve", spe)
            k.chain("dve", lambda e, r=r, sx=sx_ap: e.tensor_scalar(r, sx, 1.0 / D_, None, ALU.mult), sdve)
            k.chain("dve", lambda e, mr=mr, r=r: e.tensor_copy(mr, r), sdve)
            k.chain("dve", lambda e, r=r: e.tensor_tensor(r, r, r, ALU.mult), sdve)
            k.chain("dve", lambda e, r=r, sq=ssq_ap:
                    e.scalar_tensor_tensor(r, sq, 1.0 / D_, r, ALU.mult, ALU.subtract), sdve)
            k.chain("dve", lambda e, r=r: e.tensor_scalar(r, r, EPS, None, ALU.add), sdve)
            k.wait("act", sdve)
            k.op("act", lambda e, r=r: e.activation(r, r, AF.Sqrt), sact)
            k.wait("dve", sact)
            k.chain("dve", lambda e, r=r: e.reciprocal(r, r), sdve)
            k.chain("dve", lambda e, mr=mr, r=r: e.tensor_tensor(mr, mr, r, ALU.mult), sdve)

        def ln_rows_both():
            ln_rows(0, pB[0:1, 0:512], pC[0:1, 0:512])
            ln_rows(1, pB[0:1, 512:1024], pD[0:1, 0:512])

        def bcast_rows(tb):
            r = S1[0:1, tb * 1024 + 0:tb * 1024 + 512]
            mr = S1[0:1, tb * 1024 + 512:tb * 1024 + 1024]
            k.wait("pe", sdve)
            k.op("pe", lambda e, r=r: e.matmul(pC[:, :], onesr_f[:, :], r, start=True, stop=True))
            k.op("pe", lambda e, mr=mr: e.matmul(pD[:, :], onesr_f[:, :], mr, start=True, stop=True), spe)

        dq_tiles = []
        for _wsrc, _dqdst, _rows, _swap, _ctiles, _wid in (
                (w1n, dq1in, SH_, False, [(0, 2048), (2048, 2048)], 0),
                (w2n, dq2in, SH_, False, [(0, 2048), (2048, 2048)], 1),
                (w3n, dq3in, D_, True, [(0, SH_)], 2)):
            for _lt in range(_rows // 128):
                for (_c0, _cw) in _ctiles:
                    dq_tiles.append((_wsrc, _dqdst, _lt, _c0, _cw, _swap, _wid))
        dq_last_of = {0: 21, 1: 43, 2: 75}
        dq_ag = {0: (dq1in, dq1F, cd1), 1: (dq2in, dq2F, cd2), 2: (dq3in, dq3F, cd3)}
        dq_wt_used = [0, 0]
        dq_chain_end = {}
        dq_trans_done = {}
        dq_cursor = [0]
        qkv_ptr_evac = [0]

        def emit_dq_a(idx):
            wsrc, dqdst, lt, c0, cw, swap, wid = dq_tiles[idx]
            slot = idx % 2
            wt = Cf[:, slot * 2048:slot * 2048 + cw]
            dqn = Ef[:, 2048 + slot * 2048:2048 + slot * 2048 + cw]
            acc = Ef[:, 0:cw]
            nb = cw // 64
            if dq_wt_used[slot]:
                k.wait("sp", sdve, dq_wt_used[slot])
            k.dma("sp", wt[:, :], wsrc.ap()[lt * 128:(lt + 1) * 128, c0:c0 + cw], ld)
            k.wait("dve", ld)
            k.wait("dve", spe, dq_trans_done.get(idx - 2, 0))
            wt3 = wt[:, :].rearrange("p (b w) -> p b w", w=64)
            k.chain("dve", lambda e, wt3=wt3, nb=nb: e.tensor_reduce(
                amax_sb[:, 0:nb], wt3, AXX, ALU.max, apply_absolute_value=True), sdve)
            k.chain("dve", lambda e, nb=nb: e.tensor_scalar(
                ramax_sb[:, 0:nb], amax_sb[:, 0:nb], 1e-30, None, ALU.max), sdve)
            k.chain("dve", lambda e, nb=nb: e.reciprocal(ramax_sb[:, 0:nb], ramax_sb[:, 0:nb]), sdve)
            k.chain("dve", lambda e, nb=nb: e.tensor_scalar(
                ramax_sb[:, 0:nb], ramax_sb[:, 0:nb], GQ, None, ALU.mult), sdve)
            rb = ramax_sb[:, 0:nb].unsqueeze(2).broadcast_to([128, nb, 64])
            ab = amax_sb[:, 0:nb].unsqueeze(2).broadcast_to([128, nb, 64])
            k.chain("dve", lambda e, wt3=wt3, rb=rb: e.tensor_tensor(wt3, wt3, rb, ALU.mult), sdve)
            k.chain("dve", lambda e, a=acc, w=wt: e.tensor_scalar(
                a, w[:, :], _GB[0], _CP[1], ALU.subtract, ALU.min), sdve)
            k.chain("dve", lambda e, a=acc: e.tensor_scalar(a, a, _CP[0], None, ALU.max), sdve)
            for j in range(2, 16):
                k.chain("dve", lambda e, t=dqn, w=wt, j=j: e.tensor_scalar(
                    t, w[:, :], _GB[j - 1], _CP[j], ALU.subtract, ALU.min), sdve)
                k.chain("dve", lambda e, a=acc, t=dqn: e.tensor_tensor(a, a, t, ALU.max), sdve)
            k.chain("dve", lambda e, d=dqn, a=acc, ab=ab: e.scalar_tensor_tensor(
                d.rearrange("p (b q) -> p b q", q=64), a.rearrange("p (b q) -> p b q", q=64),
                -OFFC, ab, ALU.add, ALU.mult), sdve)
            dq_chain_end[idx] = sdve.v
            dq_wt_used[slot] = sdve.v

        def emit_dq_b(idx):
            wsrc, dqdst, lt, c0, cw, swap, wid = dq_tiles[idx]
            slot = idx % 2
            dqn = Ef[:, 2048 + slot * 2048:2048 + slot * 2048 + cw]
            ntile = cw // 128
            ng = (ntile + 3) // 4
            for g in range(ng):
                cnt = min(4, ntile - g * 4)
                k.wait("pe", sdve, dq_chain_end[idx])
                if qkv_ptr_evac[0]:
                    k.wait("pe", sact, qkv_ptr_evac[0])
                for u in range(cnt):
                    k.op("pe", lambda e, d=dqn, g=g, u=u: e.transpose(
                        ptr[:, u * 128:(u + 1) * 128],
                        d[:, (g * 4 + u) * 128:(g * 4 + u + 1) * 128], id_bf[:, :]),
                        spe if u == cnt - 1 else None)
                k.wait("act", spe)
                k.wait("act", stq)
                k.op("act", lambda e, n=cnt * 128: e.copy(tst[:, 0:n], ptr[:, 0:n]), sact)
                qkv_ptr_evac[0] = sact.v
                k.wait("pool", sact)
                for u in range(cnt):
                    ci = (c0 + (g * 4 + u) * 128) // 128
                    tidx = (ci * 32 + lt) if swap else (lt * 32 + ci)
                    k.dma("pool", dqdst.ap()[tidx * 128:(tidx + 1) * 128, :],
                          tst[:, u * 128:(u + 1) * 128], stq)
            dq_trans_done[idx] = spe.v
            if idx == dq_last_of[wid]:
                din, dgath, csem = dq_ag[wid]
                k.wait("pool", stq)
                k.op("pool", lambda e, i=din, o=dgath: e.collective_compute(
                    "AllGather", ALU.bypass, replica_groups=[list(range(NCORE))],
                    ins=[i.ap()[:, :]], outs=[o.ap()[:, :]]), csem)

        def emit_dq(n):
            for _ in range(n):
                cur = dq_cursor[0]
                if cur >= 2 and cur - 2 < len(dq_tiles):
                    if (cur - 2) not in dq_trans_done:
                        emit_dq_b(cur - 2)
                if cur < len(dq_tiles):
                    emit_dq_a(cur)
                    dq_cursor[0] = cur + 1
                elif cur - 2 >= len(dq_tiles) - 1:
                    break

        def drain_dq():
            while dq_cursor[0] < len(dq_tiles):
                emit_dq(1)
            for idx in range(max(0, len(dq_tiles) - 2), len(dq_tiles)):
                if idx not in dq_trans_done:
                    emit_dq_b(idx)

        ln_stats_pass(xT)
        ln_rows_both()
        for tb in range(2):
            bcast_rows(tb)
            k.wait("sp", sdve)
            k.wait("sp", sact)
            for dc in range(32):
                sl = dc % 2
                xf = Cf[:, sl * 1024:(sl + 1) * 1024]
                t1 = Cf[:, 2048 + sl * 512:2048 + (sl + 1) * 512]
                if dc >= 2 or tb > 0:
                    k.wait("sp", sdve)
                k.dma("sp", xf[:, 0:512],
                      xT.ap()[dc * 128:(dc + 1) * 128, tb * 512:(tb + 1) * 512], ld)
                k.wait("dve", ld)
                k.wait("dve", spe)
                dstv = A[:, dc * 1024 + tb * 512: dc * 1024 + (tb + 1) * 512]
                k.op("dve", lambda e, t1=t1, xf=xf: e.tensor_tensor(t1, xf[:, 0:512], pC[:, :], ALU.mult), sdve)
                k.op("dve", lambda e, d=dstv, t1=t1: e.tensor_tensor(d, t1, pD[:, :], ALU.subtract), sdve)

        wh_free = [0, 0]

        def load_w_pool(Wd, oc, wh):
            base = Wd.ap().rearrange("(c p) q -> p c q", p=128)
            if wh_free[wh]:
                k.wait("pool", spe, wh_free[wh])
            for q4 in range(4):
                wsrc = base[:, q4 * 8:(q4 + 1) * 8, oc * 128:(oc + 1) * 128]
                dstb = Bb[:, wh * 4096 + q4 * 1024:wh * 4096 + (q4 + 1) * 1024]
                k.dma("pool", dstb.rearrange("p (c q) -> p c q", c=8), wsrc, lp)
            return lp.v

        stage_used = [0, 0]
        wstg = Bb[:, 12288:16384].bitcast(F32)

        def load_w_hw(Wd, oc, wh):
            base = Wd.ap().rearrange("(c p) q -> p c q", p=128)
            if wh_free[wh]:
                k.wait("act", spe, wh_free[wh])
            for q4 in range(4):
                sl = q4 % 2
                stg = wstg[:, sl * 1024:(sl + 1) * 1024]
                if stage_used[sl]:
                    k.wait("sp", sact, stage_used[sl])
                wsrc = base[:, q4 * 8:(q4 + 1) * 8, oc * 128:(oc + 1) * 128]
                dv = k.dma("sp", stg.rearrange("p (c q) -> p c q", c=8), wsrc, ld)
                k.wait("act", ld, dv)
                dstb = Bb[:, wh * 4096 + q4 * 1024:wh * 4096 + (q4 + 1) * 1024]
                k.op("act", lambda e, d=dstb, s=stg: e.copy(d[:, :], s), sact)
                stage_used[sl] = sact.v
            return sact.v

        qkv_jobs = [(WqT, oc, "q") for oc in range(32)] + \
                   [(WkT, oc, "k") for oc in range(8)] + \
                   [(WvT, oc, "v") for oc in range(8)]
        pslots = [(pA, 0), (pA, 512), (pB, 0), (pB, 512)]
        slot_free = {}
        qrslot_used = [0, 0]
        rot_evac_l = [0]
        k.wait("sp", sdve)
        k.wait("pe", sdve)
        k.wait("pool", sdve)

        qkv_ready = {0: load_w_hw(qkv_jobs[0][0], qkv_jobs[0][1], 0)}
        for ji, (Wd, oc, kind) in enumerate(qkv_jobs):
            wh = ji % 2
            wready = qkv_ready.pop(ji)
            if ji + 1 < len(qkv_jobs):
                W2, oc2, _ = qkv_jobs[ji + 1]
                qkv_ready[ji + 1] = load_w_hw(W2, oc2, (ji + 1) % 2)
            for tb in range(2):
                unit = ji * 2 + tb
                sl = unit % 4
                s2 = unit % 2
                pt_, po_ = pslots[sl]
                ps = pt_[:, po_:po_ + 512]
                k.wait("pe", sact, wready)
                if sl in slot_free:
                    k.wait("pe", sact, slot_free[sl])
                for dc in range(32):
                    lhs = Bb[:, wh * 4096 + dc * 128:wh * 4096 + (dc + 1) * 128]
                    rhs = A[:, dc * 1024 + tb * 512: dc * 1024 + (tb + 1) * 512]
                    k.op("pe", lambda e, l=lhs, r=rhs, ps=ps, f=(dc == 0), la=(dc == 31):
                         e.matmul(ps, l[:, :], r, start=f, stop=la),
                         spe if dc == 31 else None)
                mm_done = spe.v
                stage_bf = Bb[:, 8192 + sl * 512: 8192 + (sl + 1) * 512]
                out_bf = Bb[:, 10240 + sl * 512: 10240 + (sl + 1) * 512]
                if kind in ("q", "k"):
                    qf32 = Tf[:, s2 * 512:(s2 + 1) * 512]
                    rotf = Tf[:, 1024 + s2 * 512:1024 + (s2 + 1) * 512]
                    t1 = Tf[:, 2048:2560]
                    t2 = Tf[:, 2560:3072]
                    k.wait("act", spe, mm_done)
                    if qrslot_used[s2]:
                        k.wait("act", sdve, qrslot_used[s2])
                    k.op("act", lambda e, s=stage_bf, ps=ps: e.copy(s[:, :], ps), sact)
                    cp1 = sact.v
                    k.op("act", lambda e, d=qf32, ps=ps: e.copy(d, ps), sact)
                    slot_free[sl] = sact.v
                    cp2 = sact.v
                    k.wait("pe", sact, cp1)
                    if rot_evac_l[0]:
                        k.wait("pe", sact, rot_evac_l[0])
                    k.op("pe", lambda e, s=stage_bf: e.matmul(pE[:, :], pt_bf[:, :], s[:, :],
                                                              start=True, stop=True), spe)
                    rot_done = spe.v
                    k.wait("act", spe, rot_done)
                    k.op("act", lambda e, d=rotf: e.copy(d, pE[:, :]), sact)
                    rot_evac_l[0] = sact.v
                    cosv = cos_sb[:, tb * 512:(tb + 1) * 512]
                    sinv = sin_sb[:, tb * 512:(tb + 1) * 512]
                    k.wait("dve", sact, cp2)
                    k.op("dve", lambda e, t1=t1, q=qf32, c=cosv: e.tensor_tensor(t1, q, c, ALU.mult), sdve)
                    k.wait("dve", sact, rot_evac_l[0])
                    k.op("dve", lambda e, t2=t2, r=rotf, s=sinv: e.tensor_tensor(t2, r, s, ALU.mult), sdve)
                    k.wait("dve", stq)
                    k.op("dve", lambda e, o=out_bf, t1=t1, t2=t2:
                         e.tensor_tensor(o[:, :], t1, t2, ALU.add), sdve)
                    qrslot_used[s2] = sdve.v
                    dst_dram = qin if kind == "q" else kin
                    k.wait("pool", sdve)
                    k.dma("pool", dst_dram.ap()[oc * 128:(oc + 1) * 128, tb * 512:(tb + 1) * 512],
                          out_bf[:, :], stq)
                else:
                    k.wait("act", spe, mm_done)
                    k.op("act", lambda e, s=stage_bf, ps=ps: e.copy(s[:, :], ps), sact)
                    slot_free[sl] = sact.v
                    k.wait("pe", sact)
                    if qkv_ptr_evac[0]:
                        k.wait("pe", sact, qkv_ptr_evac[0])
                    for tt in range(4):
                        k.op("pe", lambda e, s=stage_bf, tt=tt:
                             e.transpose(ptr[:, tt * 128:(tt + 1) * 128],
                                         s[:, tt * 128:(tt + 1) * 128], id_bf[:, :]),
                             spe if tt == 3 else None)
                    k.wait("act", spe)
                    k.wait("act", stq)
                    k.op("act", lambda e, v=out_bf: e.copy(v[:, :], ptr[:, :]), sact)
                    qkv_ptr_evac[0] = sact.v
                    k.wait("pool", sact)
                    for tt in range(4):
                        k.dma("pool", vin.ap()[oc * 1024 + tb * 512 + tt * 128:
                                               oc * 1024 + tb * 512 + (tt + 1) * 128, :],
                              out_bf[:, tt * 128:(tt + 1) * 128], stq)
            wh_free[wh] = spe.v
            if ji % 2 == 1:
                emit_dq(1)

        k.wait("pool", stq)

        def cc_a2a(ein, eout):
            def f(e, i=ein, o=eout):
                with allow_ici_all_to_all():
                    return e.collective_compute("AllToAll", ALU.bypass,
                                                replica_groups=[list(range(NCORE))],
                                                ins=[i.ap()[:, :]], outs=[o.ap()[:, :]])
            return f
        k.op("pool", cc_a2a(qin, qFl), ckv)
        k.op("pool", cc_a2a(kin, kFl), ckv)
        k.op("pool", cc_a2a(vin, vFl), ckv)

        k.wait("sp", ckv)
        k.wait("sp", stq)
        k.wait("sp", sdve)
        k.wait("pe", ckv)
        k.wait("pe", sdve)
        k.wait("pe", sact)
        fl_pc = [0]
        fl_den = [0]
        fl_recip = [0]
        fl_bcast = [0]
        q4v = Bb[:, 0:2048]
        kpref = Bb[:, 2048:6144]
        vpref = Bb[:, 6144:10240]
        attv = Bb[:, 10240:12288]
        ostage = Bb[:, 12288:12800]

        for b in range(2):
            for qb in range(8):
                nkb = 4 * qb + 4
                nbb = qb + 1
                k.wait("sp", spe)
                k.dma("sp", q4v[:, :].rearrange("p (h q) -> p h q", h=4),
                      qFl.ap()[qb * 512:(qb + 1) * 512, b * 512:(b + 1) * 512]
                      .rearrange("(h p) q -> p h q", p=128), ld)
                for r in range(nbb):
                    k.dma("sp", kpref[:, r * 512:(r + 1) * 512],
                          kFl.ap()[r * 128:(r + 1) * 128, b * 512:(b + 1) * 512], ld)
                    k.dma("sp", vpref[:, r * 512:(r + 1) * 512].rearrange("p (t h) -> p t h", t=4),
                          vFl.ap()[r * 1024 + b * 512:r * 1024 + (b + 1) * 512, :]
                          .rearrange("(t p) h -> p t h", p=128), ld)
                kv_ready = ld.v

                for hh in range(4):
                    inst = b * 4 + hh
                    qslice = q4v[:, hh * 512:(hh + 1) * 512]
                    k.wait("pe", ld, kv_ready)
                    if fl_pc[0]:
                        k.wait("pe", sdve, fl_pc[0])
                    if fl_den[0]:
                        k.wait("pe", sact, fl_den[0])
                    nbat = nkb // 2
                    exp_vals = {}
                    for bb in range(nbat):
                        sps = pA if (bb % 2 == 0) else pB
                        attsl = attv[:, (bb % 2) * 1024:((bb % 2) + 1) * 1024]
                        if bb >= 2:
                            k.wait("pe", sact, exp_vals[bb - 2])
                        for half in range(2):
                            kb = bb * 2 + half
                            lhs = kpref[:, kb * 128:(kb + 1) * 128]
                            k.op("pe", lambda e, l=lhs, q=qslice, sps=sps, half=half:
                                 e.matmul(sps[:, half * 512:(half + 1) * 512], l[:, :], q,
                                          start=True, stop=True),
                                 spe if half == 1 else None)
                        smm_done = spe.v
                        k.wait("act", spe, smm_done)
                        k.op("act", lambda e, a=attsl, sps=sps:
                             e.activation(a, sps[:, :], AF.Exp, scale=ISQ), sact)
                        exp_vals[bb] = sact.v
                        diag = (bb >= nbat - 2)
                        if diag:
                            moff = (bb - (nbat - 2)) * 1024
                            k.wait("dve", sact, exp_vals[bb])
                            k.op("dve", lambda e, a=attsl, m=mask_sb[:, moff:moff + 1024]:
                                 e.tensor_tensor(a, a, m, ALU.mult), sdve)
                            k.wait("pe", sdve)
                        else:
                            k.wait("pe", sact, exp_vals[bb])
                        for half in range(2):
                            kb = bb * 2 + half
                            vt = vpref[:, kb * 128:(kb + 1) * 128]
                            k.op("pe", lambda e, vt=vt, a=attsl, half=half, kb=kb, nkb=nkb:
                                 e.matmul(pC[:, :], vt[:, :], a[:, half * 512:(half + 1) * 512],
                                          start=(kb == 0), stop=(kb == nkb - 1)))
                            k.op("pe", lambda e, a=attsl, half=half, kb=kb, nkb=nkb:
                                 e.matmul(pD[0:1, 0:512], onesc_bf[:, :],
                                          a[:, half * 512:(half + 1) * 512],
                                          start=(kb == 0), stop=(kb == nkb - 1)),
                                 spe if half == 1 else None)
                    av_done = spe.v
                    k.wait("act", spe, av_done)
                    if fl_recip[0]:
                        k.wait("act", sdve, fl_recip[0])
                    k.wait("act", stq)
                    k.op("act", lambda e: e.copy(ostage[:, :], pC[:, :]), sact)
                    k.op("act", lambda e: e.copy(S1[0:1, 0:512], pD[0:1, 0:512]), sact)
                    fl_den[0] = sact.v
                    k.wait("dve", sact, fl_den[0])
                    if fl_bcast[0]:
                        k.wait("dve", spe, fl_bcast[0])
                    k.chain("dve", lambda e: e.reciprocal(S1[0:1, 1536:2048],
                                                          S1[0:1, 0:512]), sdve)
                    fl_recip[0] = sdve.v
                    k.wait("pe", sdve, fl_recip[0])
                    k.op("pe", lambda e: e.matmul(pE[:, :], onesr_f[:, :], S1[0:1, 1536:2048],
                                                  start=True, stop=True), spe)
                    fl_bcast[0] = spe.v
                    k.wait("dve", spe, fl_bcast[0])
                    k.op("dve", lambda e: e.tensor_tensor(ostage[:, :], ostage[:, :], pE[:, :],
                                                          ALU.mult), sdve)
                    fl_pc[0] = sdve.v
                    k.wait("pool", sdve, fl_pc[0])
                    orow = qb * 1024 + b * 512 + hh * 128
                    k.dma("pool", oin.ap()[orow:orow + 128, :], ostage[:, :], stq)
                    emit_dq(1)

        drain_dq()
        k.wait("pool", stq)
        k.op("pool", cc_a2a(oin, oFl), co)

        k.wait("pool", co)
        for dc in range(32):
            for b in range(2):
                srow = (dc // 4) * 1024 + b * 512 + (dc % 4) * 128
                k.dma("pool", A[:, dc * 1024 + b * 512: dc * 1024 + (b + 1) * 512],
                      oFl.ap()[srow:srow + 128, :], lp)
        ocat_ready = lp.v
        k.wait("pe", lp, ocat_ready)

        wo_slots = [(pA, 0), (pA, 512), (pE, 0)]
        wslot_free = {}
        wh_free[:] = [0, 0]
        k.wait("sp", co)
        wo_ready = {0: load_w_hw(WoT, 0, 0)}
        for oc in range(32):
            wh = oc % 2
            wready = wo_ready.pop(oc)
            if oc + 1 < 32:
                wo_ready[oc + 1] = load_w_hw(WoT, oc + 1, (oc + 1) % 2)
            for tb in range(2):
                sl = (oc * 2 + tb) % 3
                pt_, po_ = wo_slots[sl]
                ps = pt_[:, po_:po_ + 512]
                k.wait("pe", sact, wready)
                if sl in wslot_free:
                    k.wait("pe", sact, wslot_free[sl])
                for dc in range(32):
                    lhs = Bb[:, wh * 4096 + dc * 128:wh * 4096 + (dc + 1) * 128]
                    rhs = A[:, dc * 1024 + tb * 512: dc * 1024 + (tb + 1) * 512]
                    k.op("pe", lambda e, l=lhs, r=rhs, ps=ps, f=(dc == 0), la=(dc == 31):
                         e.matmul(ps, l[:, :], r, start=f, stop=la),
                         spe if dc == 31 else None)
                mm_done = spe.v
                stage_bf = Bb[:, 8192 + sl * 1024: 8192 + sl * 1024 + 512]
                sq = Bb[:, 8192 + sl * 1024 + 512: 8192 + (sl + 1) * 1024]
                k.wait("act", spe, mm_done)
                k.wait("act", stq)
                k.op("act", lambda e, s=stage_bf, ps=ps: e.copy(s[:, :], ps), sact)
                k.op("act", lambda e, sq=sq, ps=ps: e.activation(sq[:, :], ps, AF.Square), sact)
                wslot_free[sl] = sact.v
                k.wait("pe", sact)
                sxrow = pB[0:1, tb * 512:(tb + 1) * 512]
                sqrow = (pC if tb == 0 else pD)[0:1, 0:512]
                k.op("pe", lambda e, s=stage_bf, sx=sxrow, f=(oc == 0), la=(oc == 31):
                     e.matmul(sx, onesc_bf[:, :], s[:, :], start=f, stop=la))
                k.op("pe", lambda e, sq=sq, sr=sqrow, f=(oc == 0), la=(oc == 31):
                     e.matmul(sr, onesc_bf[:, :], sq[:, :], start=f, stop=la), spe)
                k.wait("pool", sact)
                k.dma("pool", brd.ap()[oc * 128:(oc + 1) * 128, tb * 512:(tb + 1) * 512],
                      stage_bf[:, :], stq)
            wh_free[wh] = spe.v
        ln_rows_both()

        def branch_apply(branch_dram, resid_dram, dst_dram):
            for tb in range(2):
                bcast_rows(tb)
                k.wait("sp", sdve)
                k.wait("sp", sact)
                for dc in range(32):
                    sl = dc % 2
                    aob = Bb[:, sl * 512:(sl + 1) * 512]
                    xf = Cf[:, sl * 1024:(sl + 1) * 1024]
                    t1 = Cf[:, 2048 + sl * 512:2048 + (sl + 1) * 512]
                    outv = Cf[:, 3072 + sl * 512:3072 + (sl + 1) * 512]
                    if dc >= 2 or tb > 0:
                        k.wait("sp", sdve)
                    k.dma("sp", aob[:, :], branch_dram.ap()[dc * 128:(dc + 1) * 128,
                                                            tb * 512:(tb + 1) * 512], ld)
                    k.dma("sp", xf[:, 0:512], resid_dram.ap()[dc * 128:(dc + 1) * 128,
                                                              tb * 512:(tb + 1) * 512], ld)
                    k.wait("dve", ld)
                    k.wait("dve", spe)
                    k.op("dve", lambda e, t1=t1, a=aob: e.tensor_tensor(t1, a[:, :], pC[:, :], ALU.mult), sdve)
                    k.op("dve", lambda e, t1=t1: e.tensor_tensor(t1, t1, pD[:, :], ALU.subtract), sdve)
                    k.wait("dve", stq)
                    k.op("dve", lambda e, o=outv, t1=t1, xf=xf:
                         e.tensor_tensor(o, t1, xf[:, 0:512], ALU.add), sdve)
                    k.wait("pool", sdve)
                    k.dma("pool", dst_dram.ap()[dc * 128:(dc + 1) * 128, tb * 512:(tb + 1) * 512],
                          outv[:, :], stq)
        branch_apply(brd, xT, x2d)

        k.wait("sp", stq)
        ln_stats_pass(x2d)
        ln_rows_both()
        for tb in range(2):
            bcast_rows(tb)
            k.wait("sp", sdve)
            k.wait("sp", sact)
            for dc in range(32):
                sl = dc % 2
                xf = Cf[:, sl * 1024:(sl + 1) * 1024]
                t1 = Cf[:, 2048 + sl * 512:2048 + (sl + 1) * 512]
                dstb = Bb[:, sl * 512:(sl + 1) * 512]
                if dc >= 2 or tb > 0:
                    k.wait("sp", sdve)
                k.dma("sp", xf[:, 0:512], x2d.ap()[dc * 128:(dc + 1) * 128,
                                                   tb * 512:(tb + 1) * 512], ld)
                k.wait("dve", ld)
                k.wait("dve", spe)
                k.op("dve", lambda e, t1=t1, xf=xf: e.tensor_tensor(t1, xf[:, 0:512], pC[:, :], ALU.mult), sdve)
                k.wait("dve", stq)
                k.op("dve", lambda e, d=dstb, t1=t1: e.tensor_tensor(d[:, :], t1, pD[:, :], ALU.subtract), sdve)
                k.wait("pool", sdve)
                k.dma("pool", ln2d.ap()[dc * 128:(dc + 1) * 128, tb * 512:(tb + 1) * 512],
                      dstb[:, :], stq)

        for s_ in ("sp", "pe", "pool"):
            k.wait(s_, cd1)
            k.wait(s_, cd2)
            k.wait(s_, cd3)
        for tb in range(2):
            k.wait("sp", spe)
            k.wait("sp", stq)
            k.wait("sp", sdve)
            k.wait("sp", sact)
            for dc in range(32):
                k.dma("sp", Bb[:, dc * 512:(dc + 1) * 512],
                      ln2d.ap()[dc * 128:(dc + 1) * 128, tb * 512:(tb + 1) * 512], ld)
            ln2_ready = ld.v
            k.wait("pe", ld, ln2_ready)
            k.wait("dve", spe)
            k.wait("pool", spe)
            k.wait("sp", spe)
            d1b = Bb[:, 16384:20480]
            d2b = Bb[:, 20480:24576]
            q_ready = {}
            q_freed = {}
            for hc in range(NHC):
                rk, lc = hc // LCH, hc % LCH
                base = (rk * LCH * 32 + lc * 32) * 128
                for q4 in range(4):
                    if hc >= 1:
                        k.wait("sp", spe, q_freed[(hc - 1, q4)])
                    for wsrc, dstb in ((dq1F, d1b), (dq2F, d2b)):
                        srcap = wsrc.ap()[base + q4 * 8 * 128:base + (q4 + 1) * 8 * 128, :] \
                            .rearrange("(c p) q -> p c q", p=128)
                        k.dma("sp", dstb[:, q4 * 1024:(q4 + 1) * 1024]
                              .rearrange("p (c q) -> p c q", c=8), srcap, lp)
                    q_ready[(hc, q4)] = lp.v
                gh = hc % 2
                gps = pA[:, gh * 512:(gh + 1) * 512]
                ups = pB[:, gh * 512:(gh + 1) * 512]
                if hc >= 2:
                    k.wait("pe", sdve)
                for dc in range(32):
                    if dc % 8 == 0:
                        k.wait("pe", lp, q_ready[(hc, dc // 8)])
                    l1 = d1b[:, dc * 128:(dc + 1) * 128]
                    l2 = d2b[:, dc * 128:(dc + 1) * 128]
                    rhs = Bb[:, dc * 512:(dc + 1) * 512]
                    k.op("pe", lambda e, l=l1, r=rhs, ps=gps, f=(dc == 0), la=(dc == 31):
                         e.matmul(ps, l[:, :], r, start=f, stop=la))
                    k.op("pe", lambda e, l=l2, r=rhs, ps=ups, f=(dc == 0), la=(dc == 31):
                         e.matmul(ps, l[:, :], r, start=f, stop=la),
                         spe if (dc % 8 == 7) else None)
                    if dc % 8 == 7:
                        q_freed[(hc, dc // 8)] = spe.v
                k.wait("act", spe)
                sg = Ef[:, gh * 512:(gh + 1) * 512]
                k.op("act", lambda e, sg=sg, g=gps: e.activation(sg[:, :], g, AF.Silu), sact)
                k.wait("dve", sact)
                hv = A[:, hc * 512:(hc + 1) * 512]
                k.op("dve", lambda e, h=hv, u=ups, sg=sg:
                     e.tensor_tensor(h, u, sg[:, :], ALU.mult), sdve)
            hact_done = sdve.v
            k.wait("pe", sdve, hact_done)
            down_slots = [(pA, 0), (pA, 512), (pE, 0)]
            dslot_free = {}
            oc_ready = {}
            oc_freed = {}
            base_free = spe.v

            def emit_down_load(oc):
                sbase = (oc % 2) * 11264
                k.wait("sp", spe, oc_freed.get(oc - 2, base_free))
                for rk in range(NCORE):
                    nlc = LCH if rk < 7 else (NHC - 7 * LCH)
                    srcap = bass.AP(dq3F, (rk * LCH * 32 + oc) * 128 * 128,
                                    [[128, 128], [32 * 128 * 128, nlc], [1, 128]])
                    dst = Bb[:, sbase + rk * LCH * 128:sbase + (rk * LCH + nlc) * 128] \
                        .rearrange("p (c q) -> p c q", c=nlc)
                    k.dma("sp", dst, srcap, lp)
                oc_ready[oc] = lp.v

            emit_down_load(0)
            emit_down_load(1)
            for oc in range(32):
                sbase = (oc % 2) * 11264
                sl = oc % 3
                pt_, po_ = down_slots[sl]
                ps = pt_[:, po_:po_ + 512]
                k.wait("pe", lp, oc_ready[oc])
                if sl in dslot_free:
                    k.wait("pe", sact, dslot_free[sl])
                for hc in range(NHC):
                    lhs = Bb[:, sbase + hc * 128:sbase + (hc + 1) * 128]
                    rhs = A[:, hc * 512:(hc + 1) * 512]
                    k.op("pe", lambda e, l=lhs, r=rhs, ps=ps, f=(hc == 0), la=(hc == NHC - 1):
                         e.matmul(ps, l[:, :], r, start=f, stop=la),
                         spe if hc == NHC - 1 else None)
                oc_freed[oc] = spe.v
                if oc + 2 < 32:
                    emit_down_load(oc + 2)
                stage_bf = Ef[:, sl * 1024:sl * 1024 + 512]
                sq = Ef[:, sl * 1024 + 512:(sl + 1) * 1024]
                k.wait("act", spe, oc_freed[oc])
                k.wait("act", stq)
                k.op("act", lambda e, s=stage_bf, ps=ps: e.copy(s[:, :], ps), sact)
                k.op("act", lambda e, sq=sq, ps=ps: e.activation(sq[:, :], ps, AF.Square), sact)
                dslot_free[sl] = sact.v
                k.wait("pe", sact)
                sxrow = pB[0:1, tb * 512:(tb + 1) * 512]
                sqrow = (pC if tb == 0 else pD)[0:1, 0:512]
                k.op("pe", lambda e, s=stage_bf, sx=sxrow, f=(oc == 0), la=(oc == 31):
                     e.matmul(sx, onesc_bf[:, :], s[:, :], start=f, stop=la))
                k.op("pe", lambda e, sq=sq, sr=sqrow, f=(oc == 0), la=(oc == 31):
                     e.matmul(sr, onesc_bf[:, :], sq[:, :], start=f, stop=la), spe)
                k.wait("pool", sact)
                k.dma("pool", brd.ap()[oc * 128:(oc + 1) * 128, tb * 512:(tb + 1) * 512],
                      stage_bf[:, :], stq)
            ln_rows(tb, pB[0:1, tb * 512:(tb + 1) * 512], (pC if tb == 0 else pD)[0:1, 0:512])
            bcast_rows(tb)
            k.wait("sp", sdve)
            k.wait("sp", sact)
            for dc in range(32):
                sl = dc % 2
                yb = Ef[:, 3072 + sl * 512:3072 + (sl + 1) * 512]
                xf = Cf[:, sl * 1024:(sl + 1) * 1024]
                t1 = Cf[:, 2048 + sl * 512:2048 + (sl + 1) * 512]
                outv = Cf[:, 3072 + sl * 512:3072 + (sl + 1) * 512]
                if dc >= 2:
                    k.wait("sp", sdve)
                k.wait("sp", stq)
                k.dma("sp", yb[:, :], brd.ap()[dc * 128:(dc + 1) * 128,
                                               tb * 512:(tb + 1) * 512], ld)
                k.dma("sp", xf[:, 0:512], x2d.ap()[dc * 128:(dc + 1) * 128,
                                                   tb * 512:(tb + 1) * 512], ld)
                k.wait("dve", ld)
                k.wait("dve", spe)
                k.op("dve", lambda e, t1=t1, a=yb: e.tensor_tensor(t1, a[:, :], pC[:, :], ALU.mult), sdve)
                k.op("dve", lambda e, t1=t1: e.tensor_tensor(t1, t1, pD[:, :], ALU.subtract), sdve)
                k.wait("dve", stq)
                k.op("dve", lambda e, o=outv, t1=t1, xf=xf:
                     e.tensor_tensor(o, t1, xf[:, 0:512], ALU.add), sdve)
                k.wait("pool", sdve)
                k.dma("pool", out.ap()[dc * 128:(dc + 1) * 128, tb * 512:(tb + 1) * 512],
                      outv[:, :], stq)

        for tn, tdst in taps.items():
            tsrc = interns[tn]
            k.wait("pool", stq)
            k.dma("pool", tdst.ap()[:, :].rearrange("a b -> (a b)").unsqueeze(0),
                  tsrc.ap()[:, :].rearrange("a b -> (a b)").unsqueeze(0), stq)
        k.wait("pool", stq)
        k.wait("sp", stq)

        with nc.Block() as block:
            @block.sync
            def _(eng):
                for fn in k.prog["sp"]:
                    fn(eng)

            @block.tensor
            def _(eng):
                for fn in k.prog["pe"]:
                    fn(eng)

            @block.scalar
            def _(eng):
                for fn in k.prog["act"]:
                    fn(eng)

            @block.vector
            def _(eng):
                for fn in k.prog["dve"]:
                    fn(eng)

            @block.gpsimd
            def _(eng):
                for fn in k.prog["pool"]:
                    fn(eng)

    return nc


def _rope_tables(positions):
    base = 500000.0
    inv_freq = 1.0 / base ** (np.arange(0, HD, 2, dtype=np.float32) / HD)
    factor, low_f, high_f = 32.0, 1.0, 4.0
    old_ctx = 8192
    low_wl, high_wl = old_ctx / low_f, old_ctx / high_f
    wavelen = 2 * math.pi / inv_freq
    inv_llama = np.where(wavelen > low_wl, inv_freq / factor, inv_freq)
    smooth = (old_ctx / wavelen - low_f) / (high_f - low_f)
    smoothed = (1 - smooth) * inv_llama / factor + smooth * inv_llama
    is_med = ~(wavelen < high_wl) & ~(wavelen > low_wl)
    invf = np.where(is_med, smoothed, inv_llama).astype(np.float32)
    pos = positions.astype(np.float32)
    freqs = pos[:, None] * invf[None, :]
    emb = np.concatenate([freqs, freqs], axis=-1)
    return (np.ascontiguousarray(np.cos(emb).astype(np.float32).T),
            np.ascontiguousarray(np.sin(emb).astype(np.float32).T))


def make_in_maps(L0_attn_x, L0_attn_Wq, L0_attn_Wk, L0_attn_Wv, L0_attn_Wo,
                 L0_ffn_w1, L0_ffn_w2, L0_ffn_w3):
    x = np.asarray(L0_attn_x, dtype=np.float32)
    WqTh = np.ascontiguousarray(np.asarray(L0_attn_Wq, np.float32).T)
    WkTh = np.ascontiguousarray(np.asarray(L0_attn_Wk, np.float32).T)
    WvTh = np.ascontiguousarray(np.asarray(L0_attn_Wv, np.float32).T)
    WoTh = np.ascontiguousarray(np.asarray(L0_attn_Wo, np.float32).T)
    w1 = np.asarray(L0_ffn_w1, np.float32)
    w2 = np.asarray(L0_ffn_w2, np.float32)
    w3 = np.asarray(L0_ffn_w3, np.float32)

    m = np.zeros((4, 128, 512), np.float32)
    for o in range(4):
        for kp in range(128):
            m[o, kp, :] = (np.arange(512) >= (o * 128 + kp)).astype(np.float32)
    maskb_h = np.ascontiguousarray(m.transpose(1, 0, 2).reshape(128, 2048))

    P = np.zeros((128, 128), np.float32)
    for i in range(64):
        P[i, i + 64] = -1.0
        P[i + 64, i] = 1.0
    ptm_h = np.ascontiguousarray(P.T)
    idm_h = np.eye(128, dtype=np.float32)

    in_maps = []
    for c in range(NCORE):
        tsl = slice(c * 512, (c + 1) * 512)
        xTc = np.ascontiguousarray(
            np.concatenate([x[0, tsl], x[1, tsl]], axis=0).T)
        tpos = np.concatenate([np.arange(c * 512, (c + 1) * 512)] * 2)
        cosc, sinc = _rope_tables(tpos)
        h0, h1 = c * SH_, min((c + 1) * SH_, H_)
        w1c = np.zeros((SH_, D_), np.float32); w1c[:h1 - h0] = w1[h0:h1]
        w2c = np.zeros((SH_, D_), np.float32); w2c[:h1 - h0] = w2[h0:h1]
        w3c = np.zeros((D_, SH_), np.float32); w3c[:, :h1 - h0] = w3[:, h0:h1]
        in_maps.append(dict(
            xT=xTc, cosT=cosc, sinT=sinc,
            maskb=maskb_h, ptm=ptm_h, idm=idm_h,
            WqT=WqTh, WkT=WkTh, WvT=WvTh, WoT=WoTh,
            w1n=w1c, w2n=w2c, w3n=w3c))
    return in_maps


def kernel(L0_attn_x, L0_attn_Wq, L0_attn_Wk, L0_attn_Wv, L0_attn_Wo,
           L0_ffn_w1, L0_ffn_w2, L0_ffn_w3):
    in_maps = make_in_maps(L0_attn_x, L0_attn_Wq, L0_attn_Wk, L0_attn_Wv, L0_attn_Wo,
                           L0_ffn_w1, L0_ffn_w2, L0_ffn_w3)
    nc = build_program()
    res = run_bass_kernel_spmd(nc, in_maps, core_ids=list(range(NCORE)))
    full = np.empty((B_, T_, D_), np.float32)
    for c in range(NCORE):
        oc = np.asarray(res.results[c]["out"])
        full[0, c * 512:(c + 1) * 512] = oc[:, 0:512].T
        full[1, c * 512:(c + 1) * 512] = oc[:, 512:1024].T
    return full
